# version 09
# baseline (speedup 1.0000x reference)
import numpy as np

import concourse.bass as bass
import concourse.mybir as mybir
from concourse.bass_utils import run_bass_kernel_spmd
from concourse.replica_groups import maybe_share_collective_output_space

F32 = mybir.dt.float32
BF16 = mybir.dt.bfloat16
AF = mybir.ActivationFunctionType
ALU = mybir.AluOpType
AX = mybir.AxisListType

B, S, D = 2, 8192, 2048
T = B * S
H, NOPE, ROPE, VD = 16, 128, 64, 128
QHD = NOPE + ROPE
SCALE = QHD ** -0.5
E, IFF, RLO = 8, 1024, 16
LN_EPS, RMS_EPS = 1e-5, 1e-6
NC = 8
STRIP = 2048
HSTR = 1024
NEG = -1.0e30
LATR = 2112
QG = 256

ENGS = ("te", "act", "dve", "pool", "sp")

_TAP = None
_LAST = None


class Sem:
    __slots__ = ("name", "h", "n")

    def __init__(self, name):
        self.name = name
        self.h = None
        self.n = 0


class Prog:

    def __init__(self):
        self.items = {e: [] for e in ENGS}
        self.lastw = {}
        self.serialize = False
        self.allsems = []
        self.chain = {}

    def emit(self, eng, fn, waits=(), inc=()):
        if self.serialize:
            waits = [(s, s.n) for s in self.allsems] + list(waits)
        cs = self.chain.get(eng)
        if cs is not None:
            waits = list(waits) + [(cs, cs.n)]
        ws = []
        for s, v in waits:
            if v <= 0:
                continue
            k = (eng, s.name)
            if self.lastw.get(k, -1) < v:
                self.lastw[k] = v
                ws.append((s, v))
        other = [(s, a) for s, a in inc if s is not cs]
        ii = []
        if cs is not None and not other:
            cs.n += 1
            ii.append((cs, 1))
        else:
            for s, a in other:
                s.n += a
                ii.append((s, a))
        assert len(ii) <= 1, "one sync update per instruction"
        self.items[eng].append((fn, ws, ii))


def build_nc(tap=None, detect_races=True, serialize=False):
    nc = bass.Bass("TRN2", target_bir_lowering=False,
                   detect_race_conditions=detect_races)
    P = Prog()
    sems = {}
    for name in ("dW", "dX", "dSG", "dNW", "dOUT", "cTE", "cACT", "cDVE",
                 "cPOOL", "BAR", "cc0", "cc1", "cc2", "cc3", "cc4", "cc5"):
        sems[name] = Sem(name)
    P.serialize = serialize
    P.allsems = [sems[k] for k in ("dW", "dX", "dSG", "dNW", "dOUT",
                                   "cTE", "cACT", "cDVE", "cPOOL")]
    P.chain = {"act": sems["cACT"], "dve": sems["cDVE"],
               "pool": sems["cPOOL"]}
    dW, dX, dSG, dNW, dOUT = (sems[k] for k in ("dW", "dX", "dSG", "dNW", "dOUT"))
    cTE, cACT, cDVE, cPOOL, BAR = (sems[k] for k in
                                   ("cTE", "cACT", "cDVE", "cPOOL", "BAR"))
    DMAS = [dW, dX, dSG, dNW, dOUT]
    RG = [list(range(NC))]

    def ein(name, shape):
        return nc.dram_tensor(name, shape, F32, kind="ExternalInput")

    x_in = ein("x_in", [STRIP, D])
    qa_wT = ein("qa_wT", [D, 1536])
    qa_ln = ein("qa_ln", [12, 128])
    kva_wT = ein("kva_wT", [D, 576])
    kva_ln = ein("kva_ln", [4, 128])
    qb_sT = ein("qb_sT", [1536, 2 * QHD])
    kb_sT = ein("kb_sT", [512, 2 * NOPE])
    vb_sT = ein("vb_sT", [512, 2 * VD])
    ow_sT = ein("ow_sT", [2 * VD, D])
    gate_wT = ein("gate_wT", [D, E])
    gate_wmT = ein("gate_wmT", [128, 16])
    gate_A = ein("gate_A", [RLO, D])
    gate_BT = ein("gate_BT", [RLO, E])
    gate_mBT = ein("gate_mBT", [RLO, 1])
    gup_w = ein("gup_w", [D, 2 * IFF])
    gup_A = ein("gup_A", [RLO, D])
    gup_BT = ein("gup_BT", [RLO, 2 * IFF])
    down_w = ein("down_w", [IFF, D])
    down_A = ein("down_A", [RLO, IFF])
    down_BT = ein("down_BT", [RLO, D])
    cos_pi = ein("cos_pi", [ROPE, S])
    sin_pi = ein("sin_pi", [ROPE, S])
    y_out = nc.dram_tensor("y_out", [STRIP, D], F32, kind="ExternalOutput")

    ag_sh = maybe_share_collective_output_space("AllGather", RG)
    rs_sh = maybe_share_collective_output_space("ReduceScatter", RG)
    ag1_in_c = [nc.dram_tensor(f"ag1_in{k}", [D, 512], BF16) for k in range(4)]
    ag1_out_c = [nc.dram_tensor(f"ag1_out{k}", [NC * D, 512], BF16,
                                addr_space=ag_sh) for k in range(4)]
    rs1_in = nc.dram_tensor("rs1_in", [T, D], BF16)
    rs1_out = nc.dram_tensor("rs1_out", [STRIP, D], BF16, addr_space=rs_sh)
    lat_in = nc.dram_tensor("lat_in", [LATR, STRIP], BF16)
    lat_out = nc.dram_tensor("lat_out", [NC * LATR, STRIP], BF16, addr_space=ag_sh)
    o_in_b = [nc.dram_tensor("o_in_b0", [S, D], BF16),
              nc.dram_tensor("o_in_b1", [S, D], BF16)]
    o_out_b = [nc.dram_tensor("o_out_b0", [HSTR, D], BF16, addr_space=rs_sh),
               nc.dram_tensor("o_out_b1", [HSTR, D], BF16, addr_space=rs_sh)]
    ag3_in_c = [nc.dram_tensor(f"ag3_in{k}", [D, 512], BF16) for k in range(4)]
    ag3_out_c = [nc.dram_tensor(f"ag3_out{k}", [NC * D, 512], BF16,
                                addr_space=ag_sh) for k in range(4)]
    rs4_in_c = [nc.dram_tensor(f"rs4_in{k}", [T, 512], BF16) for k in range(4)]
    rs4_out_c = [nc.dram_tensor(f"rs4_out{k}", [STRIP, 512], BF16,
                                addr_space=rs_sh) for k in range(4)]
    x2_dram = nc.dram_tensor("x2_dram", [STRIP, D], F32)
    x3_dram = nc.dram_tensor("x3_dram", [STRIP, D], F32)
    qaw_bf = nc.dram_tensor("qaw_bf", [D, 1536], BF16)
    kvaw_bf = nc.dram_tensor("kvaw_bf", [D, 576], BF16)
    qb_bf = nc.dram_tensor("qb_bf", [1536, 2 * QHD], BF16)
    kb_bf = nc.dram_tensor("kb_bf", [512, 2 * NOPE], BF16)
    vb_bf = nc.dram_tensor("vb_bf", [512, 2 * VD], BF16)
    ow_bf = nc.dram_tensor("ow_bf", [2 * VD, D], BF16)
    gup_eff = nc.dram_tensor("gup_eff", [D, 2 * IFF], BF16)
    down_eff = nc.dram_tensor("down_eff", [IFF, D], BF16)
    att_t = nc.dram_tensor("att_t", [2, 64, 128, 128], BF16)

    ar16 = nc.alloc_sbuf_tensor("ar16", [128, 85000], BF16)
    ar32 = nc.alloc_sbuf_tensor("ar32", [128, 8192], F32)
    ident = nc.alloc_sbuf_tensor("ident", [128, 128], BF16)
    ones_c = nc.alloc_sbuf_tensor("ones_c", [128, 1], BF16)
    ones_r = nc.alloc_sbuf_tensor("ones_r", [1, 128], BF16)
    geT = nc.alloc_sbuf_tensor("geT", [128, 16 * 9], BF16)
    ln_q = nc.alloc_sbuf_tensor("ln_q", [128, 12], F32)
    ln_kv = nc.alloc_sbuf_tensor("ln_kv", [128, 4], F32)
    tny = nc.alloc_sbuf_tensor("tny", [128, 48], F32)
    bcb = nc.alloc_sbuf_tensor("bcb", [1, 128], BF16)
    rs_bf = nc.alloc_sbuf_tensor("rs_bf", [1, 512], BF16)
    wcol = nc.alloc_sbuf_tensor("wcol", [128, 2], F32)

    psA = nc.alloc_psum_tensor("psA", [128, 2048], F32)
    psB = nc.alloc_psum_tensor("psB", [128, 1024], F32)
    psR = nc.alloc_psum_tensor("psR", [128, 512], F32)
    psT = nc.alloc_psum_tensor("psT", [128, 1024], BF16)

    PSTT = [(cDVE, 0)]

    def dma(sem, out_ap, in_ap, waits=()):
        if sem is dNW:
            waits = list(waits) + [(dNW, dNW.n)]
        P.emit("sp", lambda e, o=out_ap, i=in_ap: e.dma_start(out=o, in_=i),
               waits=waits, inc=[(sem, 16)])
        return sem.n

    def mm(lhsT, rhs, out, start, stop, waits=(), inc=()):
        P.emit("te", lambda e, a=lhsT, b=rhs, o=out, s1=start, s2=stop:
               e.matmul(o, a, b, start=s1, stop=s2), waits=waits, inc=inc)

    def barrier():
        flush = [(s, s.n) for s in DMAS]
        for en in ENGS:
            fw = list(flush)
            if en == "te":
                fw.append((sems["cTE"], sems["cTE"].n))
            P.emit(en, lambda e: e.nop(), waits=fw, inc=[(BAR, 1)])
        tgt = BAR.n
        for en in ENGS:
            P.emit(en, lambda e: e.nop(), waits=[(BAR, tgt)])

    def cc(kind, op, in_ap, out_ap, sem, waits=()):
        P.emit("pool",
               lambda e, k=kind, o=op, i=in_ap, u=out_ap:
               e.collective_compute(k, o, replica_groups=RG, ins=[i], outs=[u]),
               waits=waits, inc=[(sem, 1)])
        return sem.n

    P.emit("pool", lambda e: e.memset(ident[:, :], 0.0))
    P.emit("pool", lambda e: e.affine_select(
        out=ident[:, :], in_=ident[:, :], compare_op=ALU.not_equal, fill=1.0,
        base=0, pattern=[[-1, 128]], channel_multiplier=1), inc=[(cPOOL, 1)])
    P.emit("pool", lambda e: e.memset(ones_c[:, :], 1.0))
    P.emit("pool", lambda e: e.memset(ones_r[:, :], 1.0), inc=[(cPOOL, 1)])
    v_ident = cPOOL.n

    d0 = dma(dNW, ar32[0:12, 3000:3128], qa_ln[:, :])
    d0 = dma(dNW, ar32[0:4, 3128:3256], kva_ln[0:4, :])
    P.emit("dve", lambda e: e.tensor_copy(ar16[0:12, 0:128], ar32[0:12, 3000:3128]),
           waits=[(dNW, d0)])
    P.emit("dve", lambda e: e.tensor_copy(ar16[0:4, 128:256], ar32[0:4, 3128:3256]),
           inc=[(cDVE, 1)])
    v_lncast = cDVE.n
    P.emit("te", lambda e: e.transpose(psT[:, 0:12], ar16[0:12, 0:128],
                                       ident[0:12, 0:12]),
           waits=[(cDVE, v_lncast), (cPOOL, v_ident)])
    P.emit("te", lambda e: e.transpose(psT[:, 16:20], ar16[0:4, 128:256],
                                       ident[0:4, 0:4]), inc=[(cTE, 1)])
    v_lnT = cTE.n
    P.emit("dve", lambda e: e.tensor_copy(ln_q[:, :], psT[:, 0:12]),
           waits=[(cTE, v_lnT)])
    P.emit("dve", lambda e: e.tensor_copy(ln_kv[:, :], psT[:, 16:20]),
           inc=[(cDVE, 1)])
    v_ln = cDVE.n
    PSTT[0] = (cDVE, v_ln)

    gw3 = ar32[:, 0:144].rearrange("p (c n) -> p c n", n=9)
    d_gw = dma(dSG, gw3[:, :, 0:8], gate_wT[:, :].rearrange("(c p) n -> p c n", p=128))
    d_gm = dma(dSG, ar32[:, 144:160], gate_wmT[:, :])
    d_ga = dma(dNW, ar32[0:16, 272:2320], gate_A[:, :])
    d_gb = dma(dNW, ar32[0:16, 2320:2328], gate_BT[:, :])
    d_gmb = dma(dNW, ar32[0:16, 2328:2329], gate_mBT[:, :])
    P.emit("dve", lambda e: e.tensor_copy(
        gw3[:, :, 8:9], ar32[:, 144:160].rearrange("p (c o) -> p c o", o=1)),
        waits=[(dSG, d_gm), (dSG, d_gw)], inc=[(cDVE, 1)])
    v_gw = cDVE.n
    P.emit("dve", lambda e: e.tensor_copy(ar16[0:16, 384:2432], ar32[0:16, 272:2320]),
           waits=[(dNW, d_gmb)])
    P.emit("dve", lambda e: e.tensor_copy(ar16[0:16, 2432:2440], ar32[0:16, 2320:2328]))
    P.emit("dve", lambda e: e.tensor_copy(ar16[0:16, 2440:2441], ar32[0:16, 2328:2329]),
           inc=[(cDVE, 1)])
    v_gl = cDVE.n
    vr = (cDVE, 0)
    for c in range(16):
        mm(ar16[0:16, 384 + c * 128:384 + (c + 1) * 128], ar16[0:16, 2432:2441],
           psR[:, 64:73], True, True,
           waits=([(cDVE, v_gl)] if c == 0 else []) + [vr],
           inc=[(cTE, 1)])
        vt = cTE.n
        P.emit("dve", lambda e, c=c: e.scalar_tensor_tensor(
            geT[:, c * 9:(c + 1) * 9], psR[:, 64:73], 2.0,
            ar32[:, c * 9:(c + 1) * 9], ALU.mult, ALU.add),
            waits=[(cTE, vt), (cDVE, v_gw)], inc=[(cDVE, 1)])
        vr = (cDVE, cDVE.n)

    def cast_w(src, dst, rows, cols, fold=None):
        nch = rows // 128
        last_o = [0, 0]
        pre = [(cPOOL, cPOOL.n), (cDVE, cDVE.n)]
        for c in range(nch):
            par = c % 2
            stg = ar32[:, par * 3072: par * 3072 + cols]
            ob = ar16[:, 4096 + par * 3072: 4096 + par * 3072 + cols]
            dd = dma(dW, stg, src[c * 128:(c + 1) * 128, 0:cols],
                     waits=(pre if c < 2 else []) + [(dOUT, dOUT.n)])
            if fold is None:
                P.emit("pool", lambda e, s=stg, o=ob: e.tensor_copy(o, s),
                       waits=[(dW, dd), (dOUT, dOUT.n)], inc=[(cPOOL, 1)])
                vv = (cPOOL, cPOOL.n)
            else:
                fold(c, stg, ob, [(dW, dd), (dOUT, dOUT.n)])
                vv = (cDVE, cDVE.n)
            last_o[par] = dma(dOUT, dst[c * 128:(c + 1) * 128, 0:cols], ob,
                              waits=[vv])

    cast_w(qa_wT, qaw_bf, D, 1536)

    def kva_fold(c, stg, ob, w):
        P.emit("dve", lambda e, s=stg, o=ob: e.tensor_copy(o[:, 0:512], s[:, 0:512]),
               waits=w)
        P.emit("dve", lambda e, s=stg, o=ob: e.tensor_copy(
            o[:, 512:576].rearrange("p (a b) -> p a b", a=32, b=2),
            s[:, 512:576].rearrange("p (b a) -> p a b", b=2, a=32)),
            inc=[(cDVE, 1)])
    cast_w(kva_wT, kvaw_bf, D, 576, kva_fold)

    def qb_fold(c, stg, ob, w):
        for hl in range(2):
            o0 = hl * QHD
            P.emit("dve", lambda e, s=stg, o=ob, c=c, o0=o0: e.tensor_scalar(
                o[:, o0:o0 + NOPE], s[:, o0:o0 + NOPE],
                ln_q[:, c:c + 1], SCALE, ALU.mult, ALU.mult),
                waits=list(w) + [(cDVE, v_ln)])
            P.emit("dve", lambda e, s=stg, o=ob, c=c, o0=o0: e.tensor_scalar(
                o[:, o0 + NOPE:o0 + QHD].rearrange("p (a b) -> p a b", a=32, b=2),
                s[:, o0 + NOPE:o0 + QHD].rearrange("p (b a) -> p a b", b=2, a=32),
                ln_q[:, c:c + 1], SCALE, ALU.mult, ALU.mult),
                inc=[(cDVE, 1)] if hl == 1 else ())
    cast_w(qb_sT, qb_bf, 1536, 2 * QHD, qb_fold)

    def kb_fold(c, stg, ob, w):
        P.emit("dve", lambda e, s=stg, o=ob, c=c: e.tensor_scalar(
            o[:, :], s[:, :], ln_kv[:, c:c + 1], None, ALU.mult),
            waits=list(w) + [(cDVE, v_ln)], inc=[(cDVE, 1)])
    cast_w(kb_sT, kb_bf, 512, 2 * NOPE, kb_fold)
    cast_w(vb_sT, vb_bf, 512, 2 * VD, kb_fold)
    cast_w(ow_sT, ow_bf, 2 * VD, D)

    def fold_big(srcw, srcA, srcBT, acols, dst, rows):
        d_a = dma(dNW, ar32[0:16, 4096:4096 + acols], srcA[:, :])
        P.emit("dve", lambda e: e.tensor_copy(
            ar16[0:16, 10240:10240 + acols], ar32[0:16, 4096:4096 + acols]),
            waits=[(dNW, d_a), (cDVE, cDVE.n), (cPOOL, cPOOL.n)])
        d_b = dma(dNW, ar32[0:16, 6144:8192], srcBT[:, :])
        P.emit("dve", lambda e: e.tensor_copy(
            ar16[0:16, 12288:14336], ar32[0:16, 6144:8192]),
            waits=[(dNW, d_b)], inc=[(cDVE, 1)])
        vrdy = cDVE.n
        nch = rows // 128
        last_o = [0, 0]
        last_r = (cDVE, 0)
        for c in range(nch):
            par = c % 2
            stg = ar32[:, par * 2048: par * 2048 + 2048]
            ob = ar16[:, 4096 + par * 2048: 4096 + par * 2048 + 2048]
            dd = dma(dW, stg, srcw[c * 128:(c + 1) * 128, :],
                     waits=[(dOUT, dOUT.n), (cDVE, vrdy)])
            for ncb in range(4):
                mm(ar16[0:16, 10240 + c * 128:10240 + (c + 1) * 128],
                   ar16[0:16, 12288 + ncb * 512:12288 + (ncb + 1) * 512],
                   psA[:, ncb * 512:(ncb + 1) * 512], True, True,
                   waits=([(cDVE, vrdy)] if c == 0 and ncb == 0 else []) + [last_r],
                   inc=[(cTE, 1)] if ncb == 3 else ())
            vt = cTE.n
            P.emit("dve", lambda e, s=stg, o=ob: e.scalar_tensor_tensor(
                o[:, :], psA[:, :], 2.0, s[:, :], ALU.mult, ALU.add),
                waits=[(cTE, vt), (dW, dd), (dOUT, dOUT.n)], inc=[(cDVE, 1)])
            last_r = (cDVE, cDVE.n)
            last_o[par] = dma(dOUT, dst[c * 128:(c + 1) * 128, :], ob,
                              waits=[last_r])

    fold_big(gup_w, gup_A, gup_BT, D, gup_eff, D)
    fold_big(down_w, down_A, down_BT, IFF, down_eff, IFF)

    barrier()

    LNACT = [0]

    def layernorm_tile(xb, xn_out, vx, out_w=()):
        P.emit("dve", lambda e: e.reduce_sum(tny[:, 0:1], xb, axis=AX.X), waits=vx)
        P.emit("dve", lambda e: e.tensor_single_scalar(tny[:, 1:2], tny[:, 0:1],
                                                       1.0 / D, ALU.mult))
        P.emit("act", lambda e: e.activation(ar32[:, 6144:8192], xb, AF.Square,
                                             accum_out=tny[:, 2:3]),
               waits=vx, inc=[(cACT, 1)])
        vss = cACT.n
        P.emit("dve", lambda e: e.tensor_tensor(tny[:, 3:4], tny[:, 1:2],
                                                tny[:, 1:2], ALU.mult),
               waits=[(cACT, vss)])
        P.emit("dve", lambda e: e.scalar_tensor_tensor(tny[:, 4:5], tny[:, 2:3],
                                                       1.0 / D, tny[:, 3:4],
                                                       ALU.mult, ALU.subtract))
        P.emit("dve", lambda e: e.tensor_single_scalar(tny[:, 5:6], tny[:, 4:5],
                                                       LN_EPS, ALU.add),
               inc=[(cDVE, 1)])
        vv = cDVE.n
        P.emit("act", lambda e: e.sqrt(tny[:, 6:7], tny[:, 5:6]),
               waits=[(cDVE, vv)], inc=[(cACT, 1)])
        vsq = cACT.n
        LNACT[0] = vsq
        P.emit("dve", lambda e: e.reciprocal(tny[:, 7:8], tny[:, 6:7]),
               waits=[(cACT, vsq)])
        P.emit("dve", lambda e: e.tensor_scalar(xn_out, xb, tny[:, 1:2],
                                                tny[:, 7:8], ALU.subtract, ALU.mult),
               waits=list(out_w), inc=[(cDVE, 1)])
        return cDVE.n

    def transpose_tile(xn_bf, nch, dst_view, extra_w=()):
        outs = []
        for r0 in range(0, nch, 8):
            r1 = min(r0 + 8, nch)
            for c in range(r0, r1):
                mm_w = []
                if c == r0:
                    mm_w = list(extra_w) + [PSTT[0]]
                P.emit("te", lambda e, c=c, r0=r0: e.transpose(
                    psT[:, (c - r0) * 128:(c - r0 + 1) * 128],
                    xn_bf[:, c * 128:(c + 1) * 128], ident[:, :]),
                    waits=mm_w, inc=[(cTE, 1)] if c == r1 - 1 else ())
            vt = cTE.n
            P.emit("dve", lambda e, r0=r0, r1=r1: e.tensor_copy(
                dst_view(r0, r1), psT[:, 0:(r1 - r0) * 128].rearrange(
                    "p (c j) -> p c j", j=128)),
                waits=[(cTE, vt)], inc=[(cDVE, 1)])
            PSTT[0] = (cDVE, cDVE.n)
            outs.append(PSTT[0])
        return outs

    def contrib_phase(src_getter, ag_in, latf=None, to_contrib=True,
                      ag_chunks=None, ccsem=None, src_free=None):
        tr_done = []
        xn_free = [(cTE, 0), (cTE, 0)]
        cc_toks = []
        for t in range(16):
            par = t % 2
            xb, vx = src_getter(t, par)
            xnb = ar16[:, 68608 + par * 2048: 68608 + (par + 1) * 2048]
            vxn = layernorm_tile(xb, xnb, vx, out_w=[xn_free[par]])
            if src_free is not None:
                src_free[par] = [(cDVE, vxn), (cACT, LNACT[0])]
            if to_contrib:
                dv = (lambda tt: lambda r0, r1: ar16[:, 0:32768].rearrange(
                    "p (c u) -> p c u", u=2048)[:, r0:r1,
                                                tt * 128:(tt + 1) * 128])(t)
                outs = transpose_tile(xnb, 16, dv, extra_w=[(cDVE, vxn)])
                tr_done.extend(outs)
                xn_free[par] = (cTE, cTE.n)
                if ag_chunks is not None and t % 4 == 3:
                    k = t // 4
                    ddk = dma(dOUT, ag_chunks[0][k][:, :].rearrange(
                        "(c p) u -> p c u", p=128),
                        ar16[:, 0:32768].rearrange(
                            "p (c u) -> p c u", u=2048)[:, :,
                                                        k * 512:(k + 1) * 512],
                        waits=tr_done[-1:])
                    cc_toks.append(cc("AllGather", ALU.bypass,
                                      ag_chunks[0][k].ap(),
                                      ag_chunks[1][k].ap(),
                                      ccsem, waits=[(dOUT, ddk)]))
            if latf is not None:
                latf(t, xnb, vxn)
                xn_free[par] = (cTE, cTE.n)
        if ag_chunks is not None:
            return cc_toks
        if ag_in is not None:
            return dma(dOUT, ag_in[:, :].rearrange("(c p) u -> p c u", p=128),
                       ar16[:, 0:32768].rearrange("p (c u) -> p c u", u=2048),
                       waits=tr_done[-2:])
        return None

    ph1_free = {0: None, 1: None}

    def ph1_src(t, par):
        xb = ar32[:, par * 2048:(par + 1) * 2048]
        fw = ph1_free[par] or [(cDVE, cDVE.n), (cACT, cACT.n)]
        dd = dma(dX, xb, x_in[t * 128:(t + 1) * 128, :], waits=fw)
        return xb, [(dX, dd)]

    cc0_toks = contrib_phase(ph1_src, None, ag_chunks=(ag1_in_c, ag1_out_c),
                             ccsem=sems["cc0"], src_free=ph1_free)
    barrier()

    if tap == "ag1":
        _emit_dump(P, sems, ag1_out_c[0], y_out, ar16, ar32, BF16,
                   waits=[(sems["cc0"], cc0_toks[-1])])
        return _finish(nc, P, sems)

    def ffn_phase(ag_out_c, rs_in_t, ccsem, ccvals):
        for c in range(16):
            dma(dW, ar16[:, c * 2048:(c + 1) * 2048],
                gup_eff[c * 128:(c + 1) * 128, :])
        for c in range(8):
            vw = dma(dW, ar16[:, 32768 + c * 2048: 32768 + (c + 1) * 2048],
                     down_eff[c * 128:(c + 1) * 128, :])
        STG, SG, TSB, HB, HT, OB = 49152, 65536, 66560, 67584, 68608, 69632
        stage_rd = [(cTE, 0), (cTE, 0)]
        gfree = {0: (cDVE, 0), 1: (cDVE, 0), 2: (cDVE, 0), 3: (cDVE, 0)}
        down_free = [(cACT, 0), (cDVE, 0)]
        psr_free = (cDVE, 0)
        for qbi, (ck, rblk) in enumerate((k, r) for k in range(4)
                                         for r in range(8)):
            spar = qbi % 2
            coff = ck * 512
            stg = ar16[:, STG + spar * 8192: STG + (spar + 1) * 8192]
            dd = dma(dSG, stg.rearrange("p (c u) -> p c u", u=512),
                     ag_out_c[ck][rblk * 2048:(rblk + 1) * 2048,
                                  :].rearrange("(c p) u -> p c u", p=128),
                     waits=[(dSG, dSG.n), (ccsem, ccvals[ck]),
                            stage_rd[spar]] + ([(dW, vw)] if qbi == 0 else []))
            lh = stg.rearrange("p (c u) -> p c u", u=512)
            for tt in range(4):
                ti = qbi * 4 + tt
                wc = wcol[:, ti % 2: ti % 2 + 1]
                for kc in range(16):
                    mm(lh[:, kc, tt * 128:(tt + 1) * 128],
                       geT[:, kc * 9:(kc + 1) * 9],
                       psR[:, 0:9], kc == 0, kc == 15,
                       waits=([(dSG, dd), psr_free] if kc == 0 else []),
                       inc=[(cTE, 1)] if kc == 15 else ())
                v_rt = cTE.n
                vg = {}
                for ncb in (0, 2, 1, 3):
                    for kc in range(16):
                        mm(lh[:, kc, tt * 128:(tt + 1) * 128],
                           ar16[:, kc * 2048 + ncb * 512:
                                kc * 2048 + (ncb + 1) * 512],
                           psA[:, ncb * 512:(ncb + 1) * 512], kc == 0, kc == 15,
                           waits=[gfree[ncb]] if kc == 0 else (),
                           inc=[(cTE, 1)] if kc == 15 else ())
                    vg[ncb] = cTE.n
                    if ncb == 0:
                        P.emit("dve", lambda e: e.reduce_max(
                            tny[:, 8:9], psR[:, 0:8], axis=AX.X),
                            waits=[(cTE, v_rt)])
                        P.emit("dve", lambda e: e.tensor_scalar(
                            tny[:, 16:24], psR[:, 0:8], tny[:, 8:9], None,
                            ALU.is_equal))
                        P.emit("dve", lambda e: e.scalar_tensor_tensor(
                            tny[:, 24:32], tny[:, 16:24], NEG, psR[:, 0:8],
                            ALU.mult, ALU.add))
                        P.emit("dve", lambda e: e.reduce_max(
                            tny[:, 9:10], tny[:, 24:32], axis=AX.X))
                        P.emit("dve", lambda e: e.scalar_tensor_tensor(
                            tny[:, 10:11], tny[:, 8:9], -1.0, tny[:, 9:10],
                            ALU.mult, ALU.subtract), inc=[(cDVE, 1)])
                        v_r1 = cDVE.n
                        P.emit("act", lambda e: e.activation(
                            tny[:, 11:12], psR[:, 8:9], AF.Sigmoid,
                            bias=tny[:, 10:11], scale=2.0),
                            waits=[(cDVE, v_r1)], inc=[(cACT, 1)])
                        v_r2 = cACT.n
                        P.emit("dve", lambda e: e.tensor_scalar(
                            tny[:, 12:13], psR[:, 8:9], tny[:, 9:10], None,
                            ALU.is_ge), waits=[(cACT, v_r2)])
                        P.emit("dve", lambda e, wc=wc: e.tensor_tensor(
                            wc, tny[:, 11:12], tny[:, 12:13], ALU.mult),
                            inc=[(cDVE, 1)])
                        psr_free = (cDVE, cDVE.n)
                hv = {}
                for j, (gb, ub) in enumerate(((0, 2), (1, 3))):
                    P.emit("act", lambda e, j=j, gb=gb: e.activation(
                        ar16[:, SG + j * 512:SG + (j + 1) * 512],
                        psA[:, gb * 512:(gb + 1) * 512], AF.Sigmoid),
                        waits=[(cTE, vg[gb])], inc=[(cACT, 1)])
                    v_sg = cACT.n
                    P.emit("dve", lambda e, j=j, gb=gb, wc=wc:
                           e.scalar_tensor_tensor(
                               ar16[:, TSB + j * 512:TSB + (j + 1) * 512],
                               psA[:, gb * 512:(gb + 1) * 512], wc,
                               ar16[:, SG + j * 512:SG + (j + 1) * 512],
                               ALU.mult, ALU.mult),
                           waits=[(cACT, v_sg)], inc=[(cDVE, 1)])
                    gfree[gb] = (cDVE, cDVE.n)
                    P.emit("dve", lambda e, j=j, ub=ub: e.tensor_tensor(
                        ar16[:, HB + j * 512:HB + (j + 1) * 512],
                        ar16[:, TSB + j * 512:TSB + (j + 1) * 512],
                        psA[:, ub * 512:(ub + 1) * 512], ALU.mult),
                        waits=[(cTE, vg[ub])], inc=[(cDVE, 1)])
                    gfree[ub] = (cDVE, cDVE.n)
                    hv[j] = cDVE.n
                def t_half(j, hv=hv):
                    for c in range(4):
                        P.emit("te", lambda e, c=c, j=j: e.transpose(
                            psT[:, j * 512 + c * 128: j * 512 + (c + 1) * 128],
                            ar16[:, HB + j * 512 + c * 128:
                                 HB + j * 512 + (c + 1) * 128], ident[:, :]),
                            waits=([(cDVE, hv[j]), PSTT[0]] if c == 0 else []),
                            inc=[(cTE, 1)] if c == 3 else ())
                    vt = cTE.n
                    P.emit("dve", lambda e, j=j: e.tensor_copy(
                        ar16[:, HT + j * 512:HT + (j + 1) * 512],
                        psT[:, j * 512:(j + 1) * 512]),
                        waits=[(cTE, vt)], inc=[(cDVE, 1)])
                    PSTT[0] = (cDVE, cDVE.n)
                    return (cDVE, cDVE.n)

                def dn_part(jn, k0, k1, start, stop, w=()):
                    hp = jn % 2
                    for kc in range(k0, k1):
                        mm(ar16[:, HT + kc * 128:HT + (kc + 1) * 128],
                           ar16[:, 32768 + kc * 2048 + jn * 512:
                                32768 + kc * 2048 + (jn + 1) * 512],
                           psB[:, hp * 512:(hp + 1) * 512],
                           start and kc == k0, stop and kc == k1 - 1,
                           waits=list(w) if kc == k0 else (),
                           inc=[(cTE, 1)] if (stop and kc == k1 - 1) else ())

                opar = ti % 2
                ob = ar16[:, OB + opar * 2048: OB + (opar + 1) * 2048]
                ev_t = []

                def dn_evac(jn):
                    hp = jn % 2
                    v_dn = cTE.n
                    if jn % 2 == 0:
                        P.emit("act", lambda e, jn=jn, hp=hp, ob=ob: e.copy(
                            ob[:, jn * 512:(jn + 1) * 512],
                            psB[:, hp * 512:(hp + 1) * 512]),
                            waits=[(cTE, v_dn), (dOUT, dOUT.n)],
                            inc=[(cACT, 1)])
                        down_free[hp] = (cACT, cACT.n)
                    else:
                        P.emit("dve", lambda e, jn=jn, hp=hp, ob=ob: e.tensor_copy(
                            ob[:, jn * 512:(jn + 1) * 512],
                            psB[:, hp * 512:(hp + 1) * 512]),
                            waits=[(cTE, v_dn), (dOUT, dOUT.n)],
                            inc=[(cDVE, 1)])
                        down_free[hp] = (cDVE, cDVE.n)
                    ev_t.append(down_free[hp])

                hlo = t_half(0)
                dn_part(0, 0, 4, True, False, w=[hlo, down_free[0]])
                dn_part(1, 0, 4, True, False, w=[down_free[1]])
                hhi = t_half(1)
                dn_part(0, 4, 8, False, True, w=[hhi])
                dn_evac(0)
                dn_part(1, 4, 8, False, True)
                dn_evac(1)
                dn_part(2, 0, 8, True, True, w=[down_free[0]])
                dn_evac(2)
                dn_part(3, 0, 8, True, True, w=[down_free[1]])
                dn_evac(3)
                if tt == 3:
                    stage_rd[spar] = (cTE, cTE.n)
                rowb = rblk * 2048 + coff + tt * 128
                if isinstance(rs_in_t, list):
                    for kk in range(4):
                        dma(dOUT, rs_in_t[kk][rowb:rowb + 128, :],
                            ob[:, kk * 512:(kk + 1) * 512],
                            waits=ev_t if kk == 0 else ())
                else:
                    dma(dOUT, rs_in_t[rowb:rowb + 128, :], ob, waits=ev_t)
        return dOUT.n

    dd2 = ffn_phase(ag1_out_c, rs1_in, sems["cc0"], cc0_toks)
    vcc1 = cc("ReduceScatter", ALU.add, rs1_in.ap(), rs1_out.ap(), sems["cc1"],
              waits=[(dOUT, dd2)])
    barrier()

    if tap == "rs1":
        _emit_dump(P, sems, rs1_out, y_out, ar16, ar32, BF16,
                   waits=[(sems["cc1"], vcc1)])
        return _finish(nc, P, sems)

    for c in range(16):
        dma(dW, ar16[:, 34816 + c * 1536: 34816 + (c + 1) * 1536],
            qaw_bf[c * 128:(c + 1) * 128, :])
    for c in range(16):
        vwl = dma(dW, ar16[:, 59392 + c * 576: 59392 + (c + 1) * 576],
                  kvaw_bf[c * 128:(c + 1) * 128, :])

    lat_state = {"tail": (cTE, 0), "lt": (cTE, 0)}

    ph3_st = {"xb": [None, None], "rs": [None, None], "wr": [0]}

    def ph3_src(t, par):
        xb = ar32[:, par * 2048:(par + 1) * 2048]
        rsb = ar16[:, 78912 + par * 2048: 78912 + (par + 1) * 2048]
        fw = ph3_st["xb"][par] or [(cDVE, cDVE.n), (cACT, cACT.n)]
        d_a = dma(dX, xb, x_in[t * 128:(t + 1) * 128, :], waits=fw)
        d_b = dma(dX, rsb, rs1_out[t * 128:(t + 1) * 128, :],
                  waits=[(sems["cc1"], vcc1)] + (ph3_st["rs"][par] or []))
        x2 = ar32[:, 4096:6144]
        P.emit("dve", lambda e, xb=xb, rsb=rsb, x2=x2: e.scalar_tensor_tensor(
            x2, rsb, 0.5, xb, ALU.mult, ALU.add),
            waits=[(dX, d_b), (cACT, LNACT[0] if t else cACT.n),
                   (dOUT, ph3_st["wr"][0])], inc=[(cDVE, 1)])
        vx2 = cDVE.n
        ph3_st["xb"][par] = [(cDVE, vx2)]
        ph3_st["rs"][par] = [(cDVE, vx2)]
        ph3_st["wr"][0] = dma(dOUT, x2_dram[t * 128:(t + 1) * 128, :], x2,
                              waits=[(cDVE, vx2)])
        return x2, [(cDVE, vx2)]

    def ph3_lat(t, xnb, vxn):
        par = t % 2
        xT = ar16[:, 68608 + par * 2048 - 68608 + 72704 - 72704:0]
        xT = ar16[:, 70656 + par * 2048: 70656 + (par + 1) * 2048]
        touts = transpose_tile(xnb, 16,
                               lambda r0, r1, xT=xT: xT[:, r0 * 128:r1 * 128]
                               .rearrange("p (c j) -> p c j", j=128),
                               extra_w=[(cDVE, vxn)])
        lw = touts[-1:] + ([(dW, vwl)] if t == 0 else [])
        xTv = xT.rearrange("p (c j) -> p c j", j=128)
        for kc in range(16):
            for ncb in range(3):
                mm(xTv[:, kc, :],
                   ar16[:, 34816 + kc * 1536 + ncb * 512:
                        34816 + kc * 1536 + (ncb + 1) * 512],
                   psA[:, ncb * 512:(ncb + 1) * 512], kc == 0, kc == 15,
                   waits=(lw + [lat_state["lt"]]) if (kc == 0 and ncb == 0) else (),
                   inc=[(cTE, 1)] if (kc == 15 and ncb == 2) else ())
        v_qa = cTE.n
        for kc in range(16):
            mm(xTv[:, kc, :], ar16[:, 59392 + kc * 576: 59392 + kc * 576 + 512],
               psA[:, 1536:2048], kc == 0, kc == 15)
            mm(xTv[:, kc, :], ar16[:, 59392 + kc * 576 + 512: 59392 + (kc + 1) * 576],
               psB[:, 0:64], kc == 0, kc == 15,
               inc=[(cTE, 1)] if kc == 15 else ())
        v_kv = cTE.n
        P.emit("act", lambda e: e.activation(ar32[:, 6144:7680], psA[:, 0:1536],
                                             AF.Square, accum_out=tny[:, 32:33]),
               waits=[(cTE, v_qa)])
        P.emit("act", lambda e: e.activation(ar32[:, 6144:6656], psA[:, 1536:2048],
                                             AF.Square, accum_out=tny[:, 33:34]),
               waits=[(cTE, v_kv)], inc=[(cACT, 1)])
        v_ss = cACT.n
        P.emit("dve", lambda e: e.tensor_scalar(tny[:, 34:35], tny[:, 32:33],
                                                1.0 / 1536, RMS_EPS, ALU.mult,
                                                ALU.add),
               waits=[(cACT, v_ss)])
        P.emit("dve", lambda e: e.tensor_scalar(tny[:, 35:36], tny[:, 33:34],
                                                1.0 / 512, RMS_EPS, ALU.mult,
                                                ALU.add), inc=[(cDVE, 1)])
        v_v = cDVE.n
        P.emit("act", lambda e: e.sqrt(tny[:, 36:38], tny[:, 34:36]),
               waits=[(cDVE, v_v)], inc=[(cACT, 1)])
        v_sq = cACT.n
        P.emit("dve", lambda e: e.reciprocal(tny[:, 38:40], tny[:, 36:38]),
               waits=[(cACT, v_sq)])
        LT = 76800
        lt = ar16[:, LT:LT + LATR]
        P.emit("dve", lambda e: e.tensor_scalar(lt[:, 0:1536], psA[:, 0:1536],
                                                tny[:, 38:39], None, ALU.mult),
               waits=[lat_state["lt"]])
        P.emit("dve", lambda e: e.tensor_scalar(lt[:, 1536:2048], psA[:, 1536:2048],
                                                tny[:, 39:40], None, ALU.mult))
        P.emit("dve", lambda e: e.tensor_copy(lt[:, 2048:2112], psB[:, 0:64]),
               inc=[(cDVE, 1)])
        v_lt = cDVE.n
        transpose_tile(lt, 16,
                       lambda r0, r1, t=t: ar16[:, 0:32768].rearrange(
                           "p (c u) -> p c u", u=2048)[:, r0:r1,
                                                       t * 128:(t + 1) * 128],
                       extra_w=[(cDVE, v_lt)])
        P.emit("te", lambda e: e.transpose(psT[0:64, 0:128], lt[:, 2048:2112],
                                           ident[:, :]),
               waits=[PSTT[0]], inc=[(cTE, 1)])
        v_tt = cTE.n
        lat_state["lt"] = (cTE, v_tt)
        P.emit("dve", lambda e, t=t: e.tensor_copy(
            ar16[0:64, 32768 + t * 128: 32768 + (t + 1) * 128], psT[0:64, 0:128]),
            waits=[(cTE, v_tt)], inc=[(cDVE, 1)])
        PSTT[0] = (cDVE, cDVE.n)
        lat_state["tail"] = PSTT[0]

    contrib_phase(ph3_src, None, latf=ph3_lat, to_contrib=False)
    dd3a = dma(dOUT, lat_in[0:2048, :].rearrange("(c p) u -> p c u", p=128),
               ar16[:, 0:32768].rearrange("p (c u) -> p c u", u=2048),
               waits=[PSTT[0], lat_state["tail"]])
    dd3b = dma(dNW, lat_in[2048:2112, :], ar16[0:64, 32768:34816],
               waits=[PSTT[0], lat_state["tail"]])
    vcc2 = cc("AllGather", ALU.bypass, lat_in.ap(), lat_out.ap(), sems["cc2"],
              waits=[(dOUT, dd3a), (dNW, dd3b)])
    barrier()

    if tap == "x2":
        _emit_dump(P, sems, x2_dram, y_out, ar16, ar32, F32, waits=[])
        return _finish(nc, P, sems)
    if tap == "lat":
        _emit_dump(P, sems, lat_out, y_out, ar16, ar32, BF16,
                   waits=[(sems["cc2"], vcc2)])
        return _finish(nc, P, sems)

    if tap == "ser4a":
        P.serialize = True
    QBW, KBW, VBW, MSK = 0, 4608, 5632, 6656
    QF1, KF1, QF2, KF2 = 8704, 16896, 25088, 33280
    VSB, KROT = 41472, 49664
    TR = 50176
    S0A, S1A, KPE, COSW, SINW = TR, TR + 8192, TR + 16384, TR + 16896, TR + 25088
    PT2, ATTS = TR, TR + 1024
    OWS, OSL0, OBF0 = TR, TR + 4096, TR + 4608

    for c in range(12):
        dma(dW, ar16[:, QBW + c * 384: QBW + (c + 1) * 384],
            qb_bf[c * 128:(c + 1) * 128, :])
    for c in range(4):
        dma(dW, ar16[:, KBW + c * 256: KBW + (c + 1) * 256],
            kb_bf[c * 128:(c + 1) * 128, :])
        vwq = dma(dW, ar16[:, VBW + c * 256: VBW + (c + 1) * 256],
                  vb_bf[c * 128:(c + 1) * 128, :])
    P.emit("pool", lambda e: e.memset(ar16[:, MSK:MSK + 2048], 0.0))
    for mi in range(4):
        P.emit("pool", lambda e, mi=mi: e.affine_select(
            out=ar16[:, MSK + mi * 512: MSK + (mi + 1) * 512],
            in_=ar16[:, MSK + mi * 512: MSK + (mi + 1) * 512],
            compare_op=ALU.is_ge, fill=NEG, base=-(mi * 128),
            pattern=[[1, 512]], channel_multiplier=-1),
            inc=[(cPOOL, 1)] if mi == 3 else ())
    v_msk = cPOOL.n

    SHUF = []
    for a in range(16):
        SHUF += [2 * a + 1, 2 * a]

    sc_free = [(cACT, 0), (cACT, 0)]
    av_done = [(cDVE, 0), (cDVE, 0)]
    pt_rd = [(cTE, 0), (cTE, 0)]
    cc3_req = [None, None]
    cc3_tok = [None, None]

    def flush_cc3():
        for b_ in range(2):
            if cc3_req[b_] is not None and cc3_tok[b_] is None:
                cc3_tok[b_] = cc("ReduceScatter", ALU.add, o_in_b[b_].ap(),
                                 o_out_b[b_].ap(), sems["cc3"],
                                 waits=[(dOUT, cc3_req[b_])])
    rs3_last = 0
    for pair in range(4):
        hl, bb = pair % 2, pair // 2
        if tap in ("ser4c",):
            P.serialize = True
        allw = [(cTE, cTE.n), (cDVE, cDVE.n), (cACT, cACT.n),
                (cPOOL, cPOOL.n), (dOUT, dOUT.n)]
        cs_tok = (cPOOL, 0)
        for half in range(2):
            d_c = dma(dNW, ar32[0:64, half * 4096:(half + 1) * 4096],
                      cos_pi[:, half * 4096:(half + 1) * 4096],
                      waits=allw + [cs_tok])
            P.emit("pool", lambda e, h=half: e.tensor_copy(
                ar16[0:64, COSW + h * 4096: COSW + (h + 1) * 4096],
                ar32[0:64, h * 4096:(h + 1) * 4096]),
                waits=[(dNW, d_c)] + allw, inc=[(cPOOL, 1)])
            cs_tok = (cPOOL, cPOOL.n)
            d_s = dma(dNW, ar32[0:64, half * 4096:(half + 1) * 4096],
                      sin_pi[:, half * 4096:(half + 1) * 4096], waits=[cs_tok])
            P.emit("pool", lambda e, h=half: e.tensor_copy(
                ar16[0:64, SINW + h * 4096: SINW + (h + 1) * 4096],
                ar32[0:64, h * 4096:(h + 1) * 4096]),
                waits=[(dNW, d_s)], inc=[(cPOOL, 1)])
            cs_tok = (cPOOL, cPOOL.n)
        stage_rd = [(cTE, 0), (cTE, 0)]
        stage_dd = [0, 0]
        kpe_rd = (cPOOL, 0)
        d_k = 0
        ev_tok = (cDVE, 0)
        qp_tok = (cDVE, 0)
        pq_tok = (cPOOL, 0)
        for nb in range(16):
            spar = nb % 2
            rblk, coff = nb // 2, bb * 1024 + (nb % 2) * 512
            stg = ar16[:, TR + spar * 8192: TR + (spar + 1) * 8192]
            exw = [(dSG, dSG.n), stage_rd[spar], (sems["cc2"], vcc2)]
            if nb == 0:
                exw += [cs_tok]
                if pair == 0:
                    exw += [(dW, vwq), (cPOOL, v_msk)]
                else:
                    exw += [sc_free[0], sc_free[1], pt_rd[0], pt_rd[1]]
            d_m = dma(dSG, stg.rearrange("p (c u) -> p c u", u=512),
                      lat_out[rblk * LATR: rblk * LATR + 2048,
                              coff:coff + 512].rearrange("(c p) u -> p c u", p=128),
                      waits=exw)
            stage_dd[spar] = d_m
            d_k = dma(dNW, ar16[0:64, KPE:KPE + 512],
                      lat_out[rblk * LATR + 2048: rblk * LATR + 2112,
                              coff:coff + 512],
                      waits=[(dNW, d_k), kpe_rd] + (exw if nb == 0 else []))
            stgv = stg.rearrange("p (c u) -> p c u", u=512)
            c0 = nb * 512
            for kc in range(12):
                mm(ar16[:, QBW + kc * 384 + hl * QHD:
                        QBW + kc * 384 + hl * QHD + 128],
                   stgv[:, kc, :], psA[:, 0:512], kc == 0, kc == 11,
                   waits=[(dSG, d_m), ev_tok] if kc == 0 else (),
                   inc=[(cTE, 1)] if kc == 11 else ())
            v_q1 = cTE.n
            for kc in range(12):
                mm(ar16[:, QBW + kc * 384 + hl * QHD + 128:
                        QBW + kc * 384 + hl * QHD + QHD],
                   stgv[:, kc, :], psR[0:64, 0:512], kc == 0, kc == 11,
                   waits=[qp_tok] if kc == 0 else (),
                   inc=[(cTE, 1)] if kc == 11 else ())
            v_q2 = cTE.n
            for kc in range(4):
                mm(ar16[:, KBW + kc * 256 + hl * 128:
                        KBW + kc * 256 + (hl + 1) * 128],
                   stgv[:, 12 + kc, :], psA[:, 512:1024], kc == 0, kc == 3,
                   inc=[(cTE, 1)] if kc == 3 else ())
            v_k1 = cTE.n
            for ts in range(4):
                for kc in range(4):
                    mm(stgv[:, 12 + kc, ts * 128:(ts + 1) * 128],
                       ar16[:, VBW + kc * 256 + hl * 128:
                            VBW + kc * 256 + (hl + 1) * 128],
                       psA[:, 1024 + ts * 128: 1024 + (ts + 1) * 128],
                       kc == 0, kc == 3,
                       inc=[(cTE, 1)] if (ts == 3 and kc == 3) else ())
            v_v = cTE.n
            stage_rd[spar] = (cTE, v_v)
            P.emit("dve", lambda e, c0=c0: e.tensor_copy(
                ar16[:, QF1 + c0: QF1 + c0 + 512], psA[:, 0:512]),
                waits=[(cTE, v_q1)])
            P.emit("dve", lambda e, c0=c0: e.tensor_copy(
                ar16[:, KF1 + c0: KF1 + c0 + 512], psA[:, 512:1024]),
                waits=[(cTE, v_k1)])
            P.emit("dve", lambda e, c0=c0: e.tensor_copy(
                ar16[:, VSB + c0: VSB + c0 + 512], psA[:, 1024:1536]),
                waits=[(cTE, v_v)], inc=[(cDVE, 1)])
            ev_tok = (cDVE, cDVE.n)
            P.emit("dve", lambda e: e.tensor_copy(ar32[0:64, 4096:4608],
                                                  psR[0:64, 0:512]),
                   waits=[(cTE, v_q2), pq_tok], inc=[(cDVE, 1)])
            v_qp = cDVE.n
            qp_tok = (cDVE, v_qp)
            P.emit("dve", lambda e: e.stream_shuffle(ar32[0:64, 4608:5120],
                                                     ar32[0:64, 4096:4608], SHUF),
                   inc=[(cDVE, 1)])
            v_qr = cDVE.n
            P.emit("pool", lambda e, c0=c0: e.tensor_tensor(
                ar32[0:64, 5120:5632], ar32[0:64, 4096:4608],
                ar16[0:64, COSW + c0: COSW + c0 + 512], ALU.mult),
                waits=[(cDVE, v_qp)])
            P.emit("pool", lambda e, c0=c0: e.tensor_tensor(
                ar32[0:64, 5632:6144], ar32[0:64, 4608:5120],
                ar16[0:64, SINW + c0: SINW + c0 + 512], ALU.mult),
                waits=[(cDVE, v_qr)])
            P.emit("pool", lambda e, c0=c0: e.tensor_tensor(
                ar16[0:64, QF2 + c0: QF2 + c0 + 512], ar32[0:64, 5120:5632],
                ar32[0:64, 5632:6144], ALU.add), inc=[(cPOOL, 1)])
            pq_tok = (cPOOL, cPOOL.n)
            P.emit("dve", lambda e: e.stream_shuffle(
                ar16[0:64, KROT:KROT + 512], ar16[0:64, KPE:KPE + 512], SHUF),
                waits=[(dNW, d_k)], inc=[(cDVE, 1)])
            v_kr = cDVE.n
            P.emit("pool", lambda e, c0=c0: e.tensor_tensor(
                ar32[0:64, 6144:6656], ar16[0:64, KPE:KPE + 512],
                ar16[0:64, COSW + c0: COSW + c0 + 512], ALU.mult),
                waits=[(dNW, d_k)])
            P.emit("pool", lambda e, c0=c0: e.tensor_tensor(
                ar32[0:64, 6656:7168], ar16[0:64, KROT:KROT + 512],
                ar16[0:64, SINW + c0: SINW + c0 + 512], ALU.mult),
                waits=[(cDVE, v_kr)])
            P.emit("pool", lambda e, c0=c0: e.tensor_tensor(
                ar16[0:64, KF2 + c0: KF2 + c0 + 512], ar32[0:64, 6144:6656],
                ar32[0:64, 6656:7168], ALU.add), inc=[(cPOOL, 1)])
            kpe_rd = (cPOOL, cPOOL.n)
        ew = [ev_tok, pq_tok, kpe_rd, qp_tok]

        P.emit("act", lambda e: e.activation(ar16[:, S0A:S0A + 8192],
                                             ar16[:, QF1:QF1 + 8192], AF.Square),
               waits=ew + [sc_free[0], sc_free[1], pt_rd[0], pt_rd[1]])
        P.emit("act", lambda e: e.activation(ar16[0:64, S1A:S1A + 8192],
                                             ar16[0:64, QF2:QF2 + 8192], AF.Square),
               inc=[(cACT, 1)])
        v_sqq = cACT.n
        nrm_r = (cDVE, 0)
        for c in range(16):
            mm(ones_c[:, 0:1], ar16[:, S0A + c * 512: S0A + (c + 1) * 512],
               psR[0:1, 0:512], True, False,
               waits=([(cACT, v_sqq)] if c == 0 else []) + [nrm_r])
            mm(ones_c[0:64, 0:1], ar16[0:64, S1A + c * 512: S1A + (c + 1) * 512],
               psR[0:1, 0:512], False, True, inc=[(cTE, 1)])
            v_n = cTE.n
            P.emit("dve", lambda e, c=c: e.reduce_max(
                ar32[0:1, 7168 + c: 7169 + c], psR[0:1, 0:512], axis=AX.X),
                waits=[(cTE, v_n)], inc=[(cDVE, 1)])
            nrm_r = (cDVE, cDVE.n)
        P.emit("act", lambda e: e.activation(ar16[:, S0A:S0A + 8192],
                                             ar16[:, KF1:KF1 + 8192], AF.Square),
               waits=[nrm_r])
        P.emit("act", lambda e: e.activation(ar16[0:64, S1A:S1A + 8192],
                                             ar16[0:64, KF2:KF2 + 8192], AF.Square),
               inc=[(cACT, 1)])
        v_sqk = cACT.n
        for c in range(16):
            mm(ones_c[:, 0:1], ar16[:, S0A + c * 512: S0A + (c + 1) * 512],
               psR[0:1, 0:512], True, False,
               waits=([(cACT, v_sqk)] if c == 0 else []) + [nrm_r])
            mm(ones_c[0:64, 0:1], ar16[0:64, S1A + c * 512: S1A + (c + 1) * 512],
               psR[0:1, 0:512], False, True, inc=[(cTE, 1)])
            v_n = cTE.n
            P.emit("dve", lambda e, c=c: e.reduce_max(
                ar32[0:1, 7200 + c: 7201 + c], psR[0:1, 0:512], axis=AX.X),
                waits=[(cTE, v_n)], inc=[(cDVE, 1)])
            nrm_r = (cDVE, cDVE.n)
        P.emit("dve", lambda e: e.reduce_max(ar32[0:1, 7232:7233],
                                             ar32[0:1, 7168:7184], axis=AX.X))
        P.emit("dve", lambda e: e.reduce_max(ar32[0:1, 7233:7234],
                                             ar32[0:1, 7200:7216], axis=AX.X))
        P.emit("dve", lambda e: e.tensor_tensor(ar32[0:1, 7234:7235],
                                                ar32[0:1, 7232:7233],
                                                ar32[0:1, 7233:7234], ALU.mult),
               inc=[(cDVE, 1)])
        v_b2 = cDVE.n
        P.emit("act", lambda e: e.sqrt(ar32[0:1, 7235:7236], ar32[0:1, 7234:7235]),
               waits=[(cDVE, v_b2)], inc=[(cACT, 1)])
        v_bs = cACT.n
        P.emit("dve", lambda e: e.tensor_copy(bcb[0:1, 0:1], ar32[0:1, 7235:7236]),
               waits=[(cACT, v_bs)], inc=[(cDVE, 1)])
        v_bc = cDVE.n
        mm(ones_r[0:1, :], bcb[0:1, 0:1], psR[:, 0:1], True, True,
           waits=[(cDVE, v_bc), nrm_r], inc=[(cTE, 1)])
        v_bct = cTE.n
        P.emit("dve", lambda e: e.tensor_single_scalar(
            ar32[:, 7424:7425], psR[:, 0:1], -1.0, ALU.mult),
            waits=[(cTE, v_bct)], inc=[(cDVE, 1)])
        v_nb = cDVE.n
        negB = ar32[:, 7424:7425]

        if tap == "qk" and pair == 0:
            blocks = ([(QF1 + i * 2048, 128) for i in range(4)] +
                      [(KF1 + i * 2048, 128) for i in range(4)] +
                      [(QF2 + i * 2048, 64) for i in range(4)] +
                      [(KF2 + i * 2048, 64) for i in range(4)])
            prev = 0
            for i, (off, pc) in enumerate(blocks):
                P.emit("dve", lambda e, off=off, pc=pc: e.tensor_copy(
                    ar32[0:pc, 0:2048], ar16[0:pc, off:off + 2048]),
                    waits=list(ew) + [(dOUT, prev)])
                tok = (cDVE, cDVE.n)
                prev = dma(dOUT, y_out[i * 128:i * 128 + pc, :],
                           ar32[0:pc, 0:2048], waits=[tok])
            return _finish(nc, P, sems)

        if tap in ("ser4c",):
            P.serialize = False
        if tap in ("ser4b",):
            P.serialize = True
        flush_cc3()
        av_par = 0
        rsum_evac = (cDVE, 0)
        bc_free = (cDVE, 0)
        att_w = 0
        pend = None
        gpend = None

        def _av(par, kt, nks):
            mm(ar16[:, VSB + kt * 128: VSB + (kt + 1) * 128],
               ar16[:, PT2 + par * 512: PT2 + (par + 1) * 512],
               psB[:, av_par * 512:(av_par + 1) * 512],
               kt == 0, kt == nks - 1,
               waits=[sc_free[par]] + ([av_done[av_par]] if kt == 0 else []))
            mm(ones_c[:, 0:1],
               ar16[:, PT2 + par * 512: PT2 + (par + 1) * 512],
               psR[0:1, 0:512], kt == 0, kt == nks - 1,
               waits=[rsum_evac] if kt == 0 else (), inc=[(cTE, 1)])
            pt_rd[par] = (cTE, cTE.n)

        def _gend(g_old, ap_old):
            nonlocal bc_free, att_w
            mm(ones_r[0:1, :], rs_bf[0:1, 0:512], psA[:, 1024:1536], True, True,
               waits=[rsum_evac, bc_free], inc=[(cTE, 1)])
            v_bcm = cTE.n
            P.emit("dve", lambda e: e.tensor_copy(ar32[:, 4096:4608],
                                                  psA[:, 1024:1536]),
                   waits=[(cTE, v_bcm)], inc=[(cDVE, 1)])
            bc_free = (cDVE, cDVE.n)
            P.emit("dve", lambda e, ap=ap_old: e.tensor_tensor(
                ar16[:, ATTS:ATTS + 512],
                psB[:, ap * 512:(ap + 1) * 512],
                ar32[:, 4096:4608], ALU.mult),
                waits=[(dOUT, att_w)], inc=[(cDVE, 1)])
            av_done[ap_old] = (cDVE, cDVE.n)
            att_w = dma(dOUT, att_t[hl, g_old * 4:(g_old + 1) * 4].rearrange(
                "t p j -> p t j"),
                ar16[:, ATTS:ATTS + 512].rearrange("p (t j) -> p t j", j=128),
                waits=[av_done[ap_old]])

        for g in range(16):
            nks = (g + 1) * 4
            qc = g * 512
            for kt in range(nks):
                par = kt % 2
                diag = kt >= nks - 4
                scw = [sc_free[par]]
                if g == 0 and kt == 0:
                    scw.append((cDVE, v_nb))
                mm(ar16[:, KF1 + kt * 128: KF1 + (kt + 1) * 128],
                   ar16[:, QF1 + qc: QF1 + qc + 512],
                   psA[:, par * 512:(par + 1) * 512], True, False, waits=scw)
                mm(ar16[0:64, KF2 + kt * 128: KF2 + (kt + 1) * 128],
                   ar16[0:64, QF2 + qc: QF2 + qc + 512],
                   psA[:, par * 512:(par + 1) * 512], False, not diag,
                   inc=[(cTE, 1)] if not diag else ())
                if diag:
                    o = kt * 128 - g * 512
                    mm(ident[:, :], ar16[:, MSK + (o // 128) * 512:
                                         MSK + (o // 128 + 1) * 512],
                       psA[:, par * 512:(par + 1) * 512], False, True,
                       inc=[(cTE, 1)])
                v_sc = cTE.n
                P.emit("act", lambda e, par=par: e.activation(
                    ar16[:, PT2 + par * 512: PT2 + (par + 1) * 512],
                    psA[:, par * 512:(par + 1) * 512], AF.Exp,
                    bias=negB[:, 0:1]),
                    waits=[(cTE, v_sc), pt_rd[par]], inc=[(cACT, 1)])
                sc_free[par] = (cACT, cACT.n)
                if gpend is not None and kt == 1:
                    _gend(*gpend)
                    gpend = None
                if pend is not None:
                    _av(pend[0], pend[1], nks)
                pend = (par, kt)
            _av(pend[0], pend[1], nks)
            pend = None
            v_last = cTE.n
            P.emit("dve", lambda e: e.tensor_copy(ar32[0:1, 6144:6656],
                                                  psR[0:1, 0:512]),
                   waits=[(cTE, v_last)])
            P.emit("dve", lambda e: e.reciprocal(ar32[0:1, 6656:7168],
                                                 ar32[0:1, 6144:6656]))
            P.emit("dve", lambda e: e.tensor_copy(rs_bf[0:1, 0:512],
                                                  ar32[0:1, 6656:7168]),
                   inc=[(cDVE, 1)])
            rsum_evac = (cDVE, cDVE.n)
            gpend = (g, av_par)
            av_par ^= 1
        _gend(*gpend)
        gpend = None

        if tap in ("att", "att1") and pair == (0 if tap == "att" else 1):
            prev = 0
            for rb in range(4):
                for tt in range(16):
                    dd = dma(dX, ar16[:, 0:128], att_t[hl, rb * 16 + tt],
                             waits=[(dOUT, dOUT.n), (cDVE, cDVE.n)])
                    P.emit("dve", lambda e, tt=tt: e.tensor_copy(
                        ar32[:, tt * 128:(tt + 1) * 128], ar16[:, 0:128]),
                        waits=[(dX, dd)])
                tok = (cDVE, cDVE.n)
                prev = dma(dOUT, y_out[rb * 128:(rb + 1) * 128, :],
                           ar32[:, 0:2048], waits=[tok])
            return _finish(nc, P, sems)

        if tap in ("ser4b",):
            P.serialize = False
        if tap in ("ser4d",):
            P.serialize = True
        if hl == 1 and tap == "ser4d_end":
            pass
        if hl == 1:
            d_ow1 = dma(dSG, ar16[:, OWS:OWS + 2048],
                        ow_bf[0:128, :],
                        waits=[(dOUT, dOUT.n), (cTE, cTE.n),
                               (cDVE, cDVE.n), (cACT, cACT.n)])
            d_ow2 = dma(dSG, ar16[:, OWS + 2048:OWS + 4096],
                        ow_bf[128:256, :], waits=[(dOUT, dOUT.n)])
            ol_free = [(0, 0), (0, 0)]
            op_free = [(cACT, 0), (cDVE, 0)]
            oe_tok = [(cTE, 0), (cTE, 0)]
            for tt in range(64):
                par = tt % 2
                OSL = OSL0 + par * 256
                d_l0 = dma(dSG, ar16[:, OSL:OSL + 128], att_t[0, tt],
                           waits=[(dSG, dSG.n), oe_tok[par]])
                d_l1 = dma(dSG, ar16[:, OSL + 128:OSL + 256], att_t[1, tt],
                           waits=([(dSG, d_ow2)] if tt < 2 else []) +
                           [oe_tok[par]])
                for ncb in range(4):
                    mm(ar16[:, OSL:OSL + 128],
                       ar16[:, OWS + ncb * 512: OWS + (ncb + 1) * 512],
                       psA[:, ncb * 512:(ncb + 1) * 512], True, False,
                       waits=([(dSG, d_l1), op_free[0],
                               sc_free[0], sc_free[1], bc_free,
                               av_done[0], av_done[1]] if ncb == 0 else
                              ([op_free[1]] if ncb == 2 else ())))
                    mm(ar16[:, OSL + 128:OSL + 256],
                       ar16[:, OWS + 2048 + ncb * 512:
                            OWS + 2048 + (ncb + 1) * 512],
                       psA[:, ncb * 512:(ncb + 1) * 512], False, True,
                       inc=[(cTE, 1)] if ncb == 3 else ())
                v_op = cTE.n
                oe_tok[par] = (cTE, v_op)
                OBF = OBF0 + par * 2048
                P.emit("act", lambda e, OBF=OBF: e.copy(
                    ar16[:, OBF:OBF + 1024], psA[:, 0:1024]),
                    waits=[(cTE, v_op), (dOUT, dOUT.n)], inc=[(cACT, 1)])
                op_free[0] = (cACT, cACT.n)
                P.emit("dve", lambda e, OBF=OBF: e.tensor_copy(
                    ar16[:, OBF + 1024:OBF + 2048], psA[:, 1024:2048]),
                    waits=[(cTE, v_op), (dOUT, dOUT.n)], inc=[(cDVE, 1)])
                op_free[1] = (cDVE, cDVE.n)
                p0 = tt * 128
                dob = dma(dOUT, o_in_b[bb][p0:p0 + 128, :],
                          ar16[:, OBF:OBF + 2048],
                          waits=[op_free[0], op_free[1]])
                ol_free[par] = (d_l1, dob)
            sc_free = [(cACT, cACT.n), (cACT, cACT.n)]
            rs3_last = dOUT.n
            cc3_req[bb] = rs3_last
            if tap in ("ser4d",):
                P.serialize = False

    if tap in ("oin", "ser4a", "ser4b", "ser4c", "ser4d"):
        _emit_dump(P, sems, o_in_b[0], y_out, ar16, ar32, BF16,
                   waits=[(dOUT, rs3_last)])
        return _finish(nc, P, sems)

    flush_cc3()
    barrier()

    if tap == "rs3":
        _emit_dump(P, sems, o_out_b[0], y_out, ar16, ar32, BF16,
                   waits=[(sems["cc3"], cc3_tok[0])])
        return _finish(nc, P, sems)

    ph5_st = {"xb": [None, None], "rs": [None, None], "wr": [0]}

    def ph5_src(t, par):
        xb = ar32[:, par * 2048:(par + 1) * 2048]
        rsb = ar16[:, 78912 + par * 2048: 78912 + (par + 1) * 2048]
        fw = ph5_st["xb"][par] or [(cDVE, cDVE.n), (cACT, cACT.n)]
        d_a = dma(dX, xb, x2_dram[t * 128:(t + 1) * 128, :], waits=fw)
        tb = t // 8
        d_b = dma(dX, rsb, o_out_b[tb][(t % 8) * 128:(t % 8 + 1) * 128, :],
                  waits=[(sems["cc3"], cc3_tok[tb])] +
                  (ph5_st["rs"][par] or [(cDVE, cDVE.n)]))
        x3 = ar32[:, 4096:6144]
        P.emit("dve", lambda e, xb=xb, rsb=rsb, x3=x3: e.scalar_tensor_tensor(
            x3, rsb, 1.0, xb, ALU.mult, ALU.add),
            waits=[(dX, d_b), (cACT, LNACT[0] if t else cACT.n),
                   (dOUT, ph5_st["wr"][0])], inc=[(cDVE, 1)])
        vx3 = cDVE.n
        ph5_st["xb"][par] = [(cDVE, vx3)]
        ph5_st["rs"][par] = [(cDVE, vx3)]
        ph5_st["wr"][0] = dma(dOUT, x3_dram[t * 128:(t + 1) * 128, :], x3,
                              waits=[(cDVE, vx3)])
        return x3, [(cDVE, vx3)]

    cc4_toks = contrib_phase(ph5_src, None, ag_chunks=(ag3_in_c, ag3_out_c),
                             ccsem=sems["cc4"])
    barrier()

    if tap == "x3":
        _emit_dump(P, sems, x3_dram, y_out, ar16, ar32, F32, waits=[])
        return _finish(nc, P, sems)

    dd6 = ffn_phase(ag3_out_c, rs4_in_c, sems["cc4"], cc4_toks)
    vcc5_k = [cc("ReduceScatter", ALU.add, rs4_in_c[k].ap(),
                 rs4_out_c[k].ap(), sems["cc5"],
                 waits=[(dOUT, dd6)] if k == 0 else ()) for k in range(4)]
    barrier()

    if tap == "rs4":
        _emit_dump(P, sems, rs4_out_c[0], y_out, ar16, ar32, BF16,
                   waits=[(sems["cc5"], vcc5_k[0])])
        return _finish(nc, P, sems)

    stt_tok = [(cDVE, 0), (cDVE, 0)]
    ywr = [0, 0]
    for i, (k, t) in enumerate((k, t) for k in range(4) for t in range(16)):
        par = i % 2
        xb = ar32[:, par * 512:(par + 1) * 512]
        rsb = ar16[:, 68608 + par * 512: 68608 + (par + 1) * 512]
        d_a = dma(dX, xb, x3_dram[t * 128:(t + 1) * 128, k * 512:(k + 1) * 512],
                  waits=[stt_tok[par]] + ([(cDVE, cDVE.n)] if i < 2 else []))
        d_b = dma(dNW, rsb, rs4_out_c[k][t * 128:(t + 1) * 128, :],
                  waits=[(sems["cc5"], vcc5_k[k]), stt_tok[par]] +
                  ([(cDVE, cDVE.n)] if i < 2 else []))
        yb = ar32[:, 4096 + par * 512: 4096 + (par + 1) * 512]
        P.emit("dve", lambda e, rsb=rsb, xb=xb, yb=yb: e.scalar_tensor_tensor(
            yb, rsb, 0.5, xb, ALU.mult, ALU.add),
            waits=[(dX, d_a), (dNW, d_b), (dOUT, ywr[par])], inc=[(cDVE, 1)])
        stt_tok[par] = (cDVE, cDVE.n)
        ywr[par] = dma(dOUT, y_out[t * 128:(t + 1) * 128, k * 512:(k + 1) * 512],
                       yb, waits=[stt_tok[par]])

    return _finish(nc, P, sems)


def _emit_dump(P, sems, src, y_out, ar16, ar32, sdt, waits=()):
    dX, dOUT, cDVE = sems["dX"], sems["dOUT"], sems["cDVE"]
    rows = min(src.shape[0], 2048)
    cols = min(src.shape[1], 2048)
    prev_out = 0
    for t in range(rows // 128):
        if sdt == F32:
            P.emit("sp", lambda e, t=t: e.dma_start(
                out=y_out[t * 128:(t + 1) * 128, 0:cols],
                in_=src[t * 128:(t + 1) * 128, 0:cols]),
                waits=list(waits), inc=[(dOUT, 16)])
        else:
            P.emit("sp", lambda e, t=t, c=cols: e.dma_start(
                out=ar16[:, 0:c], in_=src[t * 128:(t + 1) * 128, 0:c]),
                waits=list(waits) + [(cDVE, cDVE.n)], inc=[(dX, 16)])
            dd = dX.n
            P.emit("dve", lambda e, c=cols: e.tensor_copy(ar32[:, 0:c],
                                                          ar16[:, 0:c]),
                   waits=[(dX, dd), (dOUT, prev_out)], inc=[(cDVE, 1)])
            vv = cDVE.n
            P.emit("sp", lambda e, t=t, c=cols: e.dma_start(
                out=y_out[t * 128:(t + 1) * 128, 0:c], in_=ar32[:, 0:c]),
                waits=[(cDVE, vv)], inc=[(dOUT, 16)])
            prev_out = dOUT.n


def _finish(nc, P, sems):
    for s in sems.values():
        s.h = nc.alloc_semaphore(s.name)
    P.emit("sp", lambda e: e.nop(),
           waits=[(sems["dOUT"], sems["dOUT"].n)])
    secmap = {"te": "tensor", "act": "scalar", "dve": "vector",
              "pool": "gpsimd", "sp": "sync"}
    with nc.Block() as blk:
        for en in ENGS:
            items = P.items[en]

            def body(e, items=items):
                for fn, ws, ii in items:
                    for s, v in ws:
                        e.wait_ge(s.h, v)
                    inst = fn(e)
                    for s, a in ii:
                        inst = inst.then_inc(s.h, a)
            getattr(blk, secmap[en])(body)
    return nc


def _host_inputs(inp, c):
    f32 = np.float32
    x = np.asarray(inp["L0_attn_hidden_states"], f32)
    qa_w = np.asarray(inp["L0_attn_q_a_w"], f32)
    qa_lnw = np.asarray(inp["L0_attn_q_a_ln_w"], f32)
    qb_w = np.asarray(inp["L0_attn_q_b_w"], f32)
    kva_w = np.asarray(inp["L0_attn_kv_a_w"], f32)
    kva_lnw = np.asarray(inp["L0_attn_kv_a_ln_w"], f32)
    kvb_w = np.asarray(inp["L0_attn_kv_b_w"], f32)
    o_w = np.asarray(inp["L0_attn_o_w"], f32)
    g_w = np.asarray(inp["L0_ffn_gate_weight"], f32)
    g_A = np.asarray(inp["L0_ffn_gate_lora_A"], f32)
    g_B = np.asarray(inp["L0_ffn_gate_lora_B"], f32)
    gup = np.asarray(inp["L0_ffn_gate_up_proj"], f32)
    dwn = np.asarray(inp["L0_ffn_down_proj"], f32)
    gup_A = np.asarray(inp["L0_ffn_gup_lora_A"], f32)
    gup_B = np.asarray(inp["L0_ffn_gup_lora_B"], f32)
    dwn_A = np.asarray(inp["L0_ffn_down_lora_A"], f32)
    dwn_B = np.asarray(inp["L0_ffn_down_lora_B"], f32)

    inv = 1.0 / (10000.0 ** (np.arange(0, ROPE, 2, dtype=np.float64) / ROPE))
    t = np.arange(S, dtype=np.float64)
    fr = np.outer(t, inv)
    cosl = np.cos(fr).astype(f32).T
    sinl = np.sin(fr).astype(f32).T
    cos_pi = np.repeat(cosl, 2, axis=0)
    sin_pi = np.empty((ROPE, S), f32)
    sin_pi[0::2] = -sinl
    sin_pi[1::2] = sinl

    qb3 = qb_w.reshape(H, QHD, 1536)[2 * c:2 * c + 2]
    kb3 = kvb_w.reshape(H, NOPE + VD, 512)[2 * c:2 * c + 2]
    return {
        "x_in": np.ascontiguousarray(
            np.concatenate([x[0, HSTR * c:HSTR * (c + 1)],
                            x[1, HSTR * c:HSTR * (c + 1)]], axis=0)),
        "qa_wT": np.ascontiguousarray(qa_w.T),
        "qa_ln": np.ascontiguousarray(qa_lnw.reshape(12, 128)),
        "kva_wT": np.ascontiguousarray(kva_w.T),
        "kva_ln": np.ascontiguousarray(kva_lnw.reshape(4, 128)),
        "qb_sT": np.ascontiguousarray(qb3.transpose(2, 0, 1).reshape(1536, 2 * QHD)),
        "kb_sT": np.ascontiguousarray(
            kb3[:, :NOPE].transpose(2, 0, 1).reshape(512, 2 * NOPE)),
        "vb_sT": np.ascontiguousarray(
            kb3[:, NOPE:].transpose(2, 0, 1).reshape(512, 2 * VD)),
        "ow_sT": np.ascontiguousarray(o_w[:, 2 * VD * c:2 * VD * (c + 1)].T),
        "gate_wT": np.ascontiguousarray(g_w.T),
        "gate_wmT": np.ascontiguousarray(g_w[c].reshape(16, 128).T),
        "gate_A": g_A,
        "gate_BT": np.ascontiguousarray(g_B.T),
        "gate_mBT": np.ascontiguousarray(g_B[c:c + 1].T),
        "gup_w": gup[c],
        "gup_A": gup_A[c],
        "gup_BT": np.ascontiguousarray(gup_B[c].T),
        "down_w": dwn[c],
        "down_A": dwn_A[c],
        "down_BT": np.ascontiguousarray(dwn_B[c].T),
        "cos_pi": cos_pi,
        "sin_pi": sin_pi,
    }


def kernel(**inputs):
    global _LAST
    nc = build_nc(tap=_TAP)
    in_maps = [_host_inputs(inputs, c) for c in range(NC)]
    res = run_bass_kernel_spmd(nc, in_maps, core_ids=list(range(NC)))
    _LAST = res
    y = np.empty((B, S, D), np.float32)
    for c in range(NC):
        r = res.results[c]["y_out"]
        y[0, HSTR * c:HSTR * (c + 1)] = r[:HSTR]
        y[1, HSTR * c:HSTR * (c + 1)] = r[HSTR:]
    return y


# version 10 (final)
# speedup vs baseline: 1.0602x; 1.0602x over previous
import numpy as np

import concourse.bass as bass
import concourse.mybir as mybir
from concourse.bass_utils import run_bass_kernel_spmd
from concourse.replica_groups import maybe_share_collective_output_space

F32 = mybir.dt.float32
BF16 = mybir.dt.bfloat16
AF = mybir.ActivationFunctionType
ALU = mybir.AluOpType
AX = mybir.AxisListType

B, S, D = 2, 8192, 2048
T = B * S
H, NOPE, ROPE, VD = 16, 128, 64, 128
QHD = NOPE + ROPE
SCALE = QHD ** -0.5
E, IFF, RLO = 8, 1024, 16
LN_EPS, RMS_EPS = 1e-5, 1e-6
NC = 8
STRIP = 2048
HSTR = 1024
NEG = -1.0e30
LATR = 2112
QG = 256

ENGS = ("te", "act", "dve", "pool", "sp")

_TAP = None
_LAST = None


class Sem:
    __slots__ = ("name", "h", "n")

    def __init__(self, name):
        self.name = name
        self.h = None
        self.n = 0


class Prog:

    def __init__(self):
        self.items = {e: [] for e in ENGS}
        self.lastw = {}
        self.serialize = False
        self.allsems = []
        self.chain = {}

    def emit(self, eng, fn, waits=(), inc=()):
        if self.serialize:
            waits = [(s, s.n) for s in self.allsems] + list(waits)
        cs = self.chain.get(eng)
        if cs is not None:
            waits = list(waits) + [(cs, cs.n)]
        ws = []
        for s, v in waits:
            if v <= 0:
                continue
            k = (eng, s.name)
            if self.lastw.get(k, -1) < v:
                self.lastw[k] = v
                ws.append((s, v))
        other = [(s, a) for s, a in inc if s is not cs]
        ii = []
        if cs is not None and not other:
            cs.n += 1
            ii.append((cs, 1))
        else:
            for s, a in other:
                s.n += a
                ii.append((s, a))
        assert len(ii) <= 1, "one sync update per instruction"
        self.items[eng].append((fn, ws, ii))


def build_nc(tap=None, detect_races=True, serialize=False):
    nc = bass.Bass("TRN2", target_bir_lowering=False,
                   detect_race_conditions=detect_races)
    P = Prog()
    sems = {}
    for name in ("dW", "dX", "dSG", "dNW", "dOUT", "cTE", "cACT", "cDVE",
                 "cPOOL", "BAR", "cc0", "cc1", "cc2", "cc3", "cc4", "cc5"):
        sems[name] = Sem(name)
    P.serialize = serialize
    P.allsems = [sems[k] for k in ("dW", "dX", "dSG", "dNW", "dOUT",
                                   "cTE", "cACT", "cDVE", "cPOOL")]
    P.chain = {"act": sems["cACT"], "dve": sems["cDVE"],
               "pool": sems["cPOOL"]}
    dW, dX, dSG, dNW, dOUT = (sems[k] for k in ("dW", "dX", "dSG", "dNW", "dOUT"))
    cTE, cACT, cDVE, cPOOL, BAR = (sems[k] for k in
                                   ("cTE", "cACT", "cDVE", "cPOOL", "BAR"))
    DMAS = [dW, dX, dSG, dNW, dOUT]
    RG = [list(range(NC))]

    def ein(name, shape):
        return nc.dram_tensor(name, shape, F32, kind="ExternalInput")

    x_in = ein("x_in", [STRIP, D])
    qa_wT = ein("qa_wT", [D, 1536])
    qa_ln = ein("qa_ln", [12, 128])
    kva_wT = ein("kva_wT", [D, 576])
    kva_ln = ein("kva_ln", [4, 128])
    qb_sT = ein("qb_sT", [1536, 2 * QHD])
    kb_sT = ein("kb_sT", [512, 2 * NOPE])
    vb_sT = ein("vb_sT", [512, 2 * VD])
    ow_sT = ein("ow_sT", [2 * VD, D])
    gate_wT = ein("gate_wT", [D, E])
    gate_wmT = ein("gate_wmT", [128, 16])
    gate_A = ein("gate_A", [RLO, D])
    gate_BT = ein("gate_BT", [RLO, E])
    gate_mBT = ein("gate_mBT", [RLO, 1])
    gup_w = ein("gup_w", [D, 2 * IFF])
    gup_A = ein("gup_A", [RLO, D])
    gup_BT = ein("gup_BT", [RLO, 2 * IFF])
    down_w = ein("down_w", [IFF, D])
    down_A = ein("down_A", [RLO, IFF])
    down_BT = ein("down_BT", [RLO, D])
    cos_pi = ein("cos_pi", [ROPE, S])
    sin_pi = ein("sin_pi", [ROPE, S])
    y_out = nc.dram_tensor("y_out", [STRIP, D], F32, kind="ExternalOutput")

    ag_sh = maybe_share_collective_output_space("AllGather", RG)
    rs_sh = maybe_share_collective_output_space("ReduceScatter", RG)
    ag1_in_c = [nc.dram_tensor(f"ag1_in{k}", [D, 512], BF16) for k in range(4)]
    ag1_out_c = [nc.dram_tensor(f"ag1_out{k}", [NC * D, 512], BF16,
                                addr_space=ag_sh) for k in range(4)]
    rs1_in_c = [nc.dram_tensor(f"rs1_in{k}", [NC * 512, D], BF16)
                for k in range(4)]
    rs1_out_c = [nc.dram_tensor(f"rs1_out{k}", [512, D], BF16,
                                addr_space=rs_sh) for k in range(4)]
    lat_in = nc.dram_tensor("lat_in", [LATR, STRIP], BF16)
    lat_out = nc.dram_tensor("lat_out", [NC * LATR, STRIP], BF16, addr_space=ag_sh)
    o_in_b = [nc.dram_tensor("o_in_b0", [S, D], BF16),
              nc.dram_tensor("o_in_b1", [S, D], BF16)]
    o_out_b = [nc.dram_tensor("o_out_b0", [HSTR, D], BF16, addr_space=rs_sh),
               nc.dram_tensor("o_out_b1", [HSTR, D], BF16, addr_space=rs_sh)]
    ag3_in_c = [nc.dram_tensor(f"ag3_in{k}", [D, 512], BF16) for k in range(4)]
    ag3_out_c = [nc.dram_tensor(f"ag3_out{k}", [NC * D, 512], BF16,
                                addr_space=ag_sh) for k in range(4)]
    rs4_in_c = [nc.dram_tensor(f"rs4_in{k}", [NC * 512, D], BF16)
                for k in range(4)]
    rs4_out_c = [nc.dram_tensor(f"rs4_out{k}", [512, D], BF16,
                                addr_space=rs_sh) for k in range(4)]
    x2_dram = nc.dram_tensor("x2_dram", [STRIP, D], F32)
    x3_dram = nc.dram_tensor("x3_dram", [STRIP, D], F32)
    qaw_bf = nc.dram_tensor("qaw_bf", [D, 1536], BF16)
    kvaw_bf = nc.dram_tensor("kvaw_bf", [D, 576], BF16)
    qb_bf = nc.dram_tensor("qb_bf", [1536, 2 * QHD], BF16)
    kb_bf = nc.dram_tensor("kb_bf", [512, 2 * NOPE], BF16)
    vb_bf = nc.dram_tensor("vb_bf", [512, 2 * VD], BF16)
    ow_bf = nc.dram_tensor("ow_bf", [2 * VD, D], BF16)
    gup_eff = nc.dram_tensor("gup_eff", [D, 2 * IFF], BF16)
    down_eff = nc.dram_tensor("down_eff", [IFF, D], BF16)
    att_t = nc.dram_tensor("att_t", [2, 64, 128, 128], BF16)

    ar16 = nc.alloc_sbuf_tensor("ar16", [128, 85000], BF16)
    ar32 = nc.alloc_sbuf_tensor("ar32", [128, 8192], F32)
    ident = nc.alloc_sbuf_tensor("ident", [128, 128], BF16)
    ones_c = nc.alloc_sbuf_tensor("ones_c", [128, 1], BF16)
    ones_r = nc.alloc_sbuf_tensor("ones_r", [1, 128], BF16)
    geT = nc.alloc_sbuf_tensor("geT", [128, 16 * 9], BF16)
    ln_q = nc.alloc_sbuf_tensor("ln_q", [128, 12], F32)
    ln_kv = nc.alloc_sbuf_tensor("ln_kv", [128, 4], F32)
    tny = nc.alloc_sbuf_tensor("tny", [128, 48], F32)
    bcb = nc.alloc_sbuf_tensor("bcb", [1, 128], BF16)
    rs_bf = nc.alloc_sbuf_tensor("rs_bf", [1, 512], BF16)
    wcol = nc.alloc_sbuf_tensor("wcol", [128, 2], F32)

    psA = nc.alloc_psum_tensor("psA", [128, 2048], F32)
    psB = nc.alloc_psum_tensor("psB", [128, 1024], F32)
    psR = nc.alloc_psum_tensor("psR", [128, 512], F32)
    psT = nc.alloc_psum_tensor("psT", [128, 1024], BF16)

    PSTT = [(cDVE, 0)]

    def dma(sem, out_ap, in_ap, waits=()):
        if sem is dNW:
            waits = list(waits) + [(dNW, dNW.n)]
        P.emit("sp", lambda e, o=out_ap, i=in_ap: e.dma_start(out=o, in_=i),
               waits=waits, inc=[(sem, 16)])
        return sem.n

    def mm(lhsT, rhs, out, start, stop, waits=(), inc=()):
        P.emit("te", lambda e, a=lhsT, b=rhs, o=out, s1=start, s2=stop:
               e.matmul(o, a, b, start=s1, stop=s2), waits=waits, inc=inc)

    def barrier():
        flush = [(s, s.n) for s in DMAS]
        for en in ENGS:
            fw = list(flush)
            if en == "te":
                fw.append((sems["cTE"], sems["cTE"].n))
            P.emit(en, lambda e: e.nop(), waits=fw, inc=[(BAR, 1)])
        tgt = BAR.n
        for en in ENGS:
            P.emit(en, lambda e: e.nop(), waits=[(BAR, tgt)])

    def cc(kind, op, in_ap, out_ap, sem, waits=()):
        P.emit("pool",
               lambda e, k=kind, o=op, i=in_ap, u=out_ap:
               e.collective_compute(k, o, replica_groups=RG, ins=[i], outs=[u]),
               waits=waits, inc=[(sem, 1)])
        return sem.n

    P.emit("pool", lambda e: e.memset(ident[:, :], 0.0))
    P.emit("pool", lambda e: e.affine_select(
        out=ident[:, :], in_=ident[:, :], compare_op=ALU.not_equal, fill=1.0,
        base=0, pattern=[[-1, 128]], channel_multiplier=1), inc=[(cPOOL, 1)])
    P.emit("pool", lambda e: e.memset(ones_c[:, :], 1.0))
    P.emit("pool", lambda e: e.memset(ones_r[:, :], 1.0), inc=[(cPOOL, 1)])
    v_ident = cPOOL.n

    d0 = dma(dNW, ar32[0:12, 3000:3128], qa_ln[:, :])
    d0 = dma(dNW, ar32[0:4, 3128:3256], kva_ln[0:4, :])
    P.emit("dve", lambda e: e.tensor_copy(ar16[0:12, 0:128], ar32[0:12, 3000:3128]),
           waits=[(dNW, d0)])
    P.emit("dve", lambda e: e.tensor_copy(ar16[0:4, 128:256], ar32[0:4, 3128:3256]),
           inc=[(cDVE, 1)])
    v_lncast = cDVE.n
    P.emit("te", lambda e: e.transpose(psT[:, 0:12], ar16[0:12, 0:128],
                                       ident[0:12, 0:12]),
           waits=[(cDVE, v_lncast), (cPOOL, v_ident)])
    P.emit("te", lambda e: e.transpose(psT[:, 16:20], ar16[0:4, 128:256],
                                       ident[0:4, 0:4]), inc=[(cTE, 1)])
    v_lnT = cTE.n
    P.emit("dve", lambda e: e.tensor_copy(ln_q[:, :], psT[:, 0:12]),
           waits=[(cTE, v_lnT)])
    P.emit("dve", lambda e: e.tensor_copy(ln_kv[:, :], psT[:, 16:20]),
           inc=[(cDVE, 1)])
    v_ln = cDVE.n
    PSTT[0] = (cDVE, v_ln)

    gw3 = ar32[:, 0:144].rearrange("p (c n) -> p c n", n=9)
    d_gw = dma(dSG, gw3[:, :, 0:8], gate_wT[:, :].rearrange("(c p) n -> p c n", p=128))
    d_gm = dma(dSG, ar32[:, 144:160], gate_wmT[:, :])
    d_ga = dma(dNW, ar32[0:16, 272:2320], gate_A[:, :])
    d_gb = dma(dNW, ar32[0:16, 2320:2328], gate_BT[:, :])
    d_gmb = dma(dNW, ar32[0:16, 2328:2329], gate_mBT[:, :])
    P.emit("dve", lambda e: e.tensor_copy(
        gw3[:, :, 8:9], ar32[:, 144:160].rearrange("p (c o) -> p c o", o=1)),
        waits=[(dSG, d_gm), (dSG, d_gw)], inc=[(cDVE, 1)])
    v_gw = cDVE.n
    P.emit("dve", lambda e: e.tensor_copy(ar16[0:16, 384:2432], ar32[0:16, 272:2320]),
           waits=[(dNW, d_gmb)])
    P.emit("dve", lambda e: e.tensor_copy(ar16[0:16, 2432:2440], ar32[0:16, 2320:2328]))
    P.emit("dve", lambda e: e.tensor_copy(ar16[0:16, 2440:2441], ar32[0:16, 2328:2329]),
           inc=[(cDVE, 1)])
    v_gl = cDVE.n
    vr = (cDVE, 0)
    for c in range(16):
        mm(ar16[0:16, 384 + c * 128:384 + (c + 1) * 128], ar16[0:16, 2432:2441],
           psR[:, 64:73], True, True,
           waits=([(cDVE, v_gl)] if c == 0 else []) + [vr],
           inc=[(cTE, 1)])
        vt = cTE.n
        P.emit("dve", lambda e, c=c: e.scalar_tensor_tensor(
            geT[:, c * 9:(c + 1) * 9], psR[:, 64:73], 2.0,
            ar32[:, c * 9:(c + 1) * 9], ALU.mult, ALU.add),
            waits=[(cTE, vt), (cDVE, v_gw)], inc=[(cDVE, 1)])
        vr = (cDVE, cDVE.n)

    def cast_w(src, dst, rows, cols, fold=None):
        nch = rows // 128
        last_o = [0, 0]
        pre = [(cPOOL, cPOOL.n), (cDVE, cDVE.n)]
        for c in range(nch):
            par = c % 2
            stg = ar32[:, par * 3072: par * 3072 + cols]
            ob = ar16[:, 4096 + par * 3072: 4096 + par * 3072 + cols]
            dd = dma(dW, stg, src[c * 128:(c + 1) * 128, 0:cols],
                     waits=(pre if c < 2 else []) + [(dOUT, dOUT.n)])
            if fold is None:
                P.emit("pool", lambda e, s=stg, o=ob: e.tensor_copy(o, s),
                       waits=[(dW, dd), (dOUT, dOUT.n)], inc=[(cPOOL, 1)])
                vv = (cPOOL, cPOOL.n)
            else:
                fold(c, stg, ob, [(dW, dd), (dOUT, dOUT.n)])
                vv = (cDVE, cDVE.n)
            last_o[par] = dma(dOUT, dst[c * 128:(c + 1) * 128, 0:cols], ob,
                              waits=[vv])

    cast_w(qa_wT, qaw_bf, D, 1536)

    def kva_fold(c, stg, ob, w):
        P.emit("dve", lambda e, s=stg, o=ob: e.tensor_copy(o[:, 0:512], s[:, 0:512]),
               waits=w)
        P.emit("dve", lambda e, s=stg, o=ob: e.tensor_copy(
            o[:, 512:576].rearrange("p (a b) -> p a b", a=32, b=2),
            s[:, 512:576].rearrange("p (b a) -> p a b", b=2, a=32)),
            inc=[(cDVE, 1)])
    cast_w(kva_wT, kvaw_bf, D, 576, kva_fold)

    def qb_fold(c, stg, ob, w):
        for hl in range(2):
            o0 = hl * QHD
            P.emit("dve", lambda e, s=stg, o=ob, c=c, o0=o0: e.tensor_scalar(
                o[:, o0:o0 + NOPE], s[:, o0:o0 + NOPE],
                ln_q[:, c:c + 1], SCALE, ALU.mult, ALU.mult),
                waits=list(w) + [(cDVE, v_ln)])
            P.emit("dve", lambda e, s=stg, o=ob, c=c, o0=o0: e.tensor_scalar(
                o[:, o0 + NOPE:o0 + QHD].rearrange("p (a b) -> p a b", a=32, b=2),
                s[:, o0 + NOPE:o0 + QHD].rearrange("p (b a) -> p a b", b=2, a=32),
                ln_q[:, c:c + 1], SCALE, ALU.mult, ALU.mult),
                inc=[(cDVE, 1)] if hl == 1 else ())
    cast_w(qb_sT, qb_bf, 1536, 2 * QHD, qb_fold)

    def kb_fold(c, stg, ob, w):
        P.emit("dve", lambda e, s=stg, o=ob, c=c: e.tensor_scalar(
            o[:, :], s[:, :], ln_kv[:, c:c + 1], None, ALU.mult),
            waits=list(w) + [(cDVE, v_ln)], inc=[(cDVE, 1)])
    cast_w(kb_sT, kb_bf, 512, 2 * NOPE, kb_fold)
    cast_w(vb_sT, vb_bf, 512, 2 * VD, kb_fold)
    cast_w(ow_sT, ow_bf, 2 * VD, D)

    def fold_big(srcw, srcA, srcBT, acols, dst, rows):
        d_a = dma(dNW, ar32[0:16, 4096:4096 + acols], srcA[:, :])
        P.emit("dve", lambda e: e.tensor_copy(
            ar16[0:16, 10240:10240 + acols], ar32[0:16, 4096:4096 + acols]),
            waits=[(dNW, d_a), (cDVE, cDVE.n), (cPOOL, cPOOL.n)])
        d_b = dma(dNW, ar32[0:16, 6144:8192], srcBT[:, :])
        P.emit("dve", lambda e: e.tensor_copy(
            ar16[0:16, 12288:14336], ar32[0:16, 6144:8192]),
            waits=[(dNW, d_b)], inc=[(cDVE, 1)])
        vrdy = cDVE.n
        nch = rows // 128
        last_o = [0, 0]
        last_r = (cDVE, 0)
        for c in range(nch):
            par = c % 2
            stg = ar32[:, par * 2048: par * 2048 + 2048]
            ob = ar16[:, 4096 + par * 2048: 4096 + par * 2048 + 2048]
            dd = dma(dW, stg, srcw[c * 128:(c + 1) * 128, :],
                     waits=[(dOUT, dOUT.n), (cDVE, vrdy)])
            for ncb in range(4):
                mm(ar16[0:16, 10240 + c * 128:10240 + (c + 1) * 128],
                   ar16[0:16, 12288 + ncb * 512:12288 + (ncb + 1) * 512],
                   psA[:, ncb * 512:(ncb + 1) * 512], True, True,
                   waits=([(cDVE, vrdy)] if c == 0 and ncb == 0 else []) + [last_r],
                   inc=[(cTE, 1)] if ncb == 3 else ())
            vt = cTE.n
            P.emit("dve", lambda e, s=stg, o=ob: e.scalar_tensor_tensor(
                o[:, :], psA[:, :], 2.0, s[:, :], ALU.mult, ALU.add),
                waits=[(cTE, vt), (dW, dd), (dOUT, dOUT.n)], inc=[(cDVE, 1)])
            last_r = (cDVE, cDVE.n)
            last_o[par] = dma(dOUT, dst[c * 128:(c + 1) * 128, :], ob,
                              waits=[last_r])

    fold_big(gup_w, gup_A, gup_BT, D, gup_eff, D)
    fold_big(down_w, down_A, down_BT, IFF, down_eff, IFF)

    barrier()

    LNACT = [0]

    def layernorm_tile(xb, xn_out, vx, out_w=()):
        P.emit("dve", lambda e: e.reduce_sum(tny[:, 0:1], xb, axis=AX.X), waits=vx)
        P.emit("dve", lambda e: e.tensor_single_scalar(tny[:, 1:2], tny[:, 0:1],
                                                       1.0 / D, ALU.mult))
        P.emit("act", lambda e: e.activation(ar32[:, 6144:8192], xb, AF.Square,
                                             accum_out=tny[:, 2:3]),
               waits=vx, inc=[(cACT, 1)])
        vss = cACT.n
        P.emit("dve", lambda e: e.tensor_tensor(tny[:, 3:4], tny[:, 1:2],
                                                tny[:, 1:2], ALU.mult),
               waits=[(cACT, vss)])
        P.emit("dve", lambda e: e.scalar_tensor_tensor(tny[:, 4:5], tny[:, 2:3],
                                                       1.0 / D, tny[:, 3:4],
                                                       ALU.mult, ALU.subtract))
        P.emit("dve", lambda e: e.tensor_single_scalar(tny[:, 5:6], tny[:, 4:5],
                                                       LN_EPS, ALU.add),
               inc=[(cDVE, 1)])
        vv = cDVE.n
        P.emit("act", lambda e: e.sqrt(tny[:, 6:7], tny[:, 5:6]),
               waits=[(cDVE, vv)], inc=[(cACT, 1)])
        vsq = cACT.n
        LNACT[0] = vsq
        P.emit("dve", lambda e: e.reciprocal(tny[:, 7:8], tny[:, 6:7]),
               waits=[(cACT, vsq)])
        P.emit("dve", lambda e: e.tensor_scalar(xn_out, xb, tny[:, 1:2],
                                                tny[:, 7:8], ALU.subtract, ALU.mult),
               waits=list(out_w), inc=[(cDVE, 1)])
        return cDVE.n

    def transpose_tile(xn_bf, nch, dst_view, extra_w=()):
        outs = []
        for r0 in range(0, nch, 8):
            r1 = min(r0 + 8, nch)
            for c in range(r0, r1):
                mm_w = []
                if c == r0:
                    mm_w = list(extra_w) + [PSTT[0]]
                P.emit("te", lambda e, c=c, r0=r0: e.transpose(
                    psT[:, (c - r0) * 128:(c - r0 + 1) * 128],
                    xn_bf[:, c * 128:(c + 1) * 128], ident[:, :]),
                    waits=mm_w, inc=[(cTE, 1)] if c == r1 - 1 else ())
            vt = cTE.n
            P.emit("dve", lambda e, r0=r0, r1=r1: e.tensor_copy(
                dst_view(r0, r1), psT[:, 0:(r1 - r0) * 128].rearrange(
                    "p (c j) -> p c j", j=128)),
                waits=[(cTE, vt)], inc=[(cDVE, 1)])
            PSTT[0] = (cDVE, cDVE.n)
            outs.append(PSTT[0])
        return outs

    def contrib_phase(src_getter, ag_in, latf=None, to_contrib=True,
                      ag_chunks=None, ccsem=None, src_free=None):
        tr_done = []
        xn_free = [(cTE, 0), (cTE, 0)]
        cc_toks = []
        for t in range(16):
            par = t % 2
            xb, vx = src_getter(t, par)
            xnb = ar16[:, 68608 + par * 2048: 68608 + (par + 1) * 2048]
            vxn = layernorm_tile(xb, xnb, vx, out_w=[xn_free[par]])
            if src_free is not None:
                src_free[par] = [(cDVE, vxn), (cACT, LNACT[0])]
            if to_contrib:
                dv = (lambda tt: lambda r0, r1: ar16[:, 0:32768].rearrange(
                    "p (c u) -> p c u", u=2048)[:, r0:r1,
                                                tt * 128:(tt + 1) * 128])(t)
                outs = transpose_tile(xnb, 16, dv, extra_w=[(cDVE, vxn)])
                tr_done.extend(outs)
                xn_free[par] = (cTE, cTE.n)
                if ag_chunks is not None and t % 4 == 3:
                    k = t // 4
                    ddk = dma(dOUT, ag_chunks[0][k][:, :].rearrange(
                        "(c p) u -> p c u", p=128),
                        ar16[:, 0:32768].rearrange(
                            "p (c u) -> p c u", u=2048)[:, :,
                                                        k * 512:(k + 1) * 512],
                        waits=tr_done[-1:])
                    cc_toks.append(cc("AllGather", ALU.bypass,
                                      ag_chunks[0][k].ap(),
                                      ag_chunks[1][k].ap(),
                                      ccsem, waits=[(dOUT, ddk)]))
            if latf is not None:
                latf(t, xnb, vxn)
                xn_free[par] = (cTE, cTE.n)
        if ag_chunks is not None:
            return cc_toks
        if ag_in is not None:
            return dma(dOUT, ag_in[:, :].rearrange("(c p) u -> p c u", p=128),
                       ar16[:, 0:32768].rearrange("p (c u) -> p c u", u=2048),
                       waits=tr_done[-2:])
        return None

    ph1_free = {0: None, 1: None}

    def ph1_src(t, par):
        xb = ar32[:, par * 2048:(par + 1) * 2048]
        fw = ph1_free[par] or [(cDVE, cDVE.n), (cACT, cACT.n)]
        dd = dma(dX, xb, x_in[t * 128:(t + 1) * 128, :], waits=fw)
        return xb, [(dX, dd)]

    cc0_toks = contrib_phase(ph1_src, None, ag_chunks=(ag1_in_c, ag1_out_c),
                             ccsem=sems["cc0"], src_free=ph1_free)
    barrier()

    if tap == "ag1":
        _emit_dump(P, sems, ag1_out_c[0], y_out, ar16, ar32, BF16,
                   waits=[(sems["cc0"], cc0_toks[-1])])
        return _finish(nc, P, sems)

    def ffn_phase(ag_out_c, rs_in_t, ccsem, ccvals):
        for c in range(16):
            dma(dW, ar16[:, c * 2048:(c + 1) * 2048],
                gup_eff[c * 128:(c + 1) * 128, :])
        for c in range(8):
            vw = dma(dW, ar16[:, 32768 + c * 2048: 32768 + (c + 1) * 2048],
                     down_eff[c * 128:(c + 1) * 128, :])
        STG, SG, TSB, HB, HT, OB = 49152, 65536, 66560, 67584, 68608, 69632
        chunk_dd = [0, 0, 0, 0]
        stage_rd = [(cTE, 0), (cTE, 0)]
        gfree = {0: (cDVE, 0), 1: (cDVE, 0), 2: (cDVE, 0), 3: (cDVE, 0)}
        down_free = [(cACT, 0), (cDVE, 0)]
        psr_free = (cDVE, 0)
        for qbi, (ck, rblk) in enumerate((k, r) for k in range(4)
                                         for r in range(8)):
            spar = qbi % 2
            coff = ck * 512
            stg = ar16[:, STG + spar * 8192: STG + (spar + 1) * 8192]
            dd = dma(dSG, stg.rearrange("p (c u) -> p c u", u=512),
                     ag_out_c[ck][rblk * 2048:(rblk + 1) * 2048,
                                  :].rearrange("(c p) u -> p c u", p=128),
                     waits=[(dSG, dSG.n), (ccsem, ccvals[ck]),
                            stage_rd[spar]] + ([(dW, vw)] if qbi == 0 else []))
            lh = stg.rearrange("p (c u) -> p c u", u=512)
            for tt in range(4):
                ti = qbi * 4 + tt
                wc = wcol[:, ti % 2: ti % 2 + 1]
                for kc in range(16):
                    mm(lh[:, kc, tt * 128:(tt + 1) * 128],
                       geT[:, kc * 9:(kc + 1) * 9],
                       psR[:, 0:9], kc == 0, kc == 15,
                       waits=([(dSG, dd), psr_free] if kc == 0 else []),
                       inc=[(cTE, 1)] if kc == 15 else ())
                v_rt = cTE.n
                vg = {}
                for ncb in (0, 2, 1, 3):
                    for kc in range(16):
                        mm(lh[:, kc, tt * 128:(tt + 1) * 128],
                           ar16[:, kc * 2048 + ncb * 512:
                                kc * 2048 + (ncb + 1) * 512],
                           psA[:, ncb * 512:(ncb + 1) * 512], kc == 0, kc == 15,
                           waits=[gfree[ncb]] if kc == 0 else (),
                           inc=[(cTE, 1)] if kc == 15 else ())
                    vg[ncb] = cTE.n
                    if ncb == 0:
                        P.emit("dve", lambda e: e.reduce_max(
                            tny[:, 8:9], psR[:, 0:8], axis=AX.X),
                            waits=[(cTE, v_rt)])
                        P.emit("dve", lambda e: e.tensor_scalar(
                            tny[:, 16:24], psR[:, 0:8], tny[:, 8:9], None,
                            ALU.is_equal))
                        P.emit("dve", lambda e: e.scalar_tensor_tensor(
                            tny[:, 24:32], tny[:, 16:24], NEG, psR[:, 0:8],
                            ALU.mult, ALU.add))
                        P.emit("dve", lambda e: e.reduce_max(
                            tny[:, 9:10], tny[:, 24:32], axis=AX.X))
                        P.emit("dve", lambda e: e.scalar_tensor_tensor(
                            tny[:, 10:11], tny[:, 8:9], -1.0, tny[:, 9:10],
                            ALU.mult, ALU.subtract), inc=[(cDVE, 1)])
                        v_r1 = cDVE.n
                        P.emit("act", lambda e: e.activation(
                            tny[:, 11:12], psR[:, 8:9], AF.Sigmoid,
                            bias=tny[:, 10:11], scale=2.0),
                            waits=[(cDVE, v_r1)], inc=[(cACT, 1)])
                        v_r2 = cACT.n
                        P.emit("dve", lambda e: e.tensor_scalar(
                            tny[:, 12:13], psR[:, 8:9], tny[:, 9:10], None,
                            ALU.is_ge), waits=[(cACT, v_r2)])
                        P.emit("dve", lambda e, wc=wc: e.tensor_tensor(
                            wc, tny[:, 11:12], tny[:, 12:13], ALU.mult),
                            inc=[(cDVE, 1)])
                        psr_free = (cDVE, cDVE.n)
                hv = {}
                for j, (gb, ub) in enumerate(((0, 2), (1, 3))):
                    P.emit("act", lambda e, j=j, gb=gb: e.activation(
                        ar16[:, SG + j * 512:SG + (j + 1) * 512],
                        psA[:, gb * 512:(gb + 1) * 512], AF.Sigmoid),
                        waits=[(cTE, vg[gb])], inc=[(cACT, 1)])
                    v_sg = cACT.n
                    P.emit("dve", lambda e, j=j, gb=gb, wc=wc:
                           e.scalar_tensor_tensor(
                               ar16[:, TSB + j * 512:TSB + (j + 1) * 512],
                               psA[:, gb * 512:(gb + 1) * 512], wc,
                               ar16[:, SG + j * 512:SG + (j + 1) * 512],
                               ALU.mult, ALU.mult),
                           waits=[(cACT, v_sg)], inc=[(cDVE, 1)])
                    gfree[gb] = (cDVE, cDVE.n)
                    P.emit("dve", lambda e, j=j, ub=ub: e.tensor_tensor(
                        ar16[:, HB + j * 512:HB + (j + 1) * 512],
                        ar16[:, TSB + j * 512:TSB + (j + 1) * 512],
                        psA[:, ub * 512:(ub + 1) * 512], ALU.mult),
                        waits=[(cTE, vg[ub])], inc=[(cDVE, 1)])
                    gfree[ub] = (cDVE, cDVE.n)
                    hv[j] = cDVE.n
                def t_half(j, hv=hv):
                    for c in range(4):
                        P.emit("te", lambda e, c=c, j=j: e.transpose(
                            psT[:, j * 512 + c * 128: j * 512 + (c + 1) * 128],
                            ar16[:, HB + j * 512 + c * 128:
                                 HB + j * 512 + (c + 1) * 128], ident[:, :]),
                            waits=([(cDVE, hv[j]), PSTT[0]] if c == 0 else []),
                            inc=[(cTE, 1)] if c == 3 else ())
                    vt = cTE.n
                    P.emit("dve", lambda e, j=j: e.tensor_copy(
                        ar16[:, HT + j * 512:HT + (j + 1) * 512],
                        psT[:, j * 512:(j + 1) * 512]),
                        waits=[(cTE, vt)], inc=[(cDVE, 1)])
                    PSTT[0] = (cDVE, cDVE.n)
                    return (cDVE, cDVE.n)

                def dn_part(jn, k0, k1, start, stop, w=()):
                    hp = jn % 2
                    for kc in range(k0, k1):
                        mm(ar16[:, HT + kc * 128:HT + (kc + 1) * 128],
                           ar16[:, 32768 + kc * 2048 + jn * 512:
                                32768 + kc * 2048 + (jn + 1) * 512],
                           psB[:, hp * 512:(hp + 1) * 512],
                           start and kc == k0, stop and kc == k1 - 1,
                           waits=list(w) if kc == k0 else (),
                           inc=[(cTE, 1)] if (stop and kc == k1 - 1) else ())

                opar = ti % 2
                ob = ar16[:, OB + opar * 2048: OB + (opar + 1) * 2048]
                ev_t = []

                def dn_evac(jn):
                    hp = jn % 2
                    v_dn = cTE.n
                    if jn % 2 == 0:
                        P.emit("act", lambda e, jn=jn, hp=hp, ob=ob: e.copy(
                            ob[:, jn * 512:(jn + 1) * 512],
                            psB[:, hp * 512:(hp + 1) * 512]),
                            waits=[(cTE, v_dn), (dOUT, dOUT.n)],
                            inc=[(cACT, 1)])
                        down_free[hp] = (cACT, cACT.n)
                    else:
                        P.emit("dve", lambda e, jn=jn, hp=hp, ob=ob: e.tensor_copy(
                            ob[:, jn * 512:(jn + 1) * 512],
                            psB[:, hp * 512:(hp + 1) * 512]),
                            waits=[(cTE, v_dn), (dOUT, dOUT.n)],
                            inc=[(cDVE, 1)])
                        down_free[hp] = (cDVE, cDVE.n)
                    ev_t.append(down_free[hp])

                hlo = t_half(0)
                dn_part(0, 0, 4, True, False, w=[hlo, down_free[0]])
                dn_part(1, 0, 4, True, False, w=[down_free[1]])
                hhi = t_half(1)
                dn_part(0, 4, 8, False, True, w=[hhi])
                dn_evac(0)
                dn_part(1, 4, 8, False, True)
                dn_evac(1)
                dn_part(2, 0, 8, True, True, w=[down_free[0]])
                dn_evac(2)
                dn_part(3, 0, 8, True, True, w=[down_free[1]])
                dn_evac(3)
                if tt == 3:
                    stage_rd[spar] = (cTE, cTE.n)
                rowb = rblk * 512 + tt * 128
                dma(dOUT, rs_in_t[ck][rowb:rowb + 128, :], ob, waits=ev_t)
            if rblk == 7:
                chunk_dd[ck] = dOUT.n
        return chunk_dd

    dd2_k = ffn_phase(ag1_out_c, rs1_in_c, sems["cc0"], cc0_toks)
    vcc1_k = [cc("ReduceScatter", ALU.add, rs1_in_c[k].ap(),
                 rs1_out_c[k].ap(), sems["cc1"],
                 waits=[(dOUT, dd2_k[k])]) for k in range(4)]
    barrier()

    if tap == "rs1":
        _emit_dump(P, sems, rs1_out_c[0], y_out, ar16, ar32, BF16,
                   waits=[(sems["cc1"], vcc1_k[0])])
        return _finish(nc, P, sems)

    for c in range(16):
        dma(dW, ar16[:, 34816 + c * 1536: 34816 + (c + 1) * 1536],
            qaw_bf[c * 128:(c + 1) * 128, :])
    for c in range(16):
        vwl = dma(dW, ar16[:, 59392 + c * 576: 59392 + (c + 1) * 576],
                  kvaw_bf[c * 128:(c + 1) * 128, :])

    lat_state = {"tail": (cTE, 0), "lt": (cTE, 0)}

    ph3_free = {0: None, 1: None}
    ph3_st = {"wr": [0, 0], "rs": [None, None]}

    def ph3_src(t, par):
        xb = ar32[:, par * 2048:(par + 1) * 2048]
        rsb = ar16[:, 78912 + par * 2048: 78912 + (par + 1) * 2048]
        fw = ph3_free[par] or [(cDVE, cDVE.n), (cACT, cACT.n)]
        d_a = dma(dX, xb, x_in[t * 128:(t + 1) * 128, :],
                  waits=fw + [(dOUT, ph3_st["wr"][par])])
        d_b = dma(dX, rsb, rs1_out_c[t // 4][(t % 4) * 128:(t % 4 + 1) * 128, :],
                  waits=[(sems["cc1"], vcc1_k[t // 4])] +
                  (ph3_st["rs"][par] or []))
        P.emit("dve", lambda e, xb=xb, rsb=rsb: e.scalar_tensor_tensor(
            xb, rsb, 0.5, xb, ALU.mult, ALU.add),
            waits=[(dX, d_b)], inc=[(cDVE, 1)])
        vx2 = cDVE.n
        ph3_st["rs"][par] = [(cDVE, vx2)]
        ph3_st["wr"][par] = dma(dOUT, x2_dram[t * 128:(t + 1) * 128, :], xb,
                                waits=[(cDVE, vx2)])
        return xb, [(cDVE, vx2)]

    def ph3_lat(t, xnb, vxn):
        par = t % 2
        xT = ar16[:, 68608 + par * 2048 - 68608 + 72704 - 72704:0]
        xT = ar16[:, 70656 + par * 2048: 70656 + (par + 1) * 2048]
        touts = transpose_tile(xnb, 16,
                               lambda r0, r1, xT=xT: xT[:, r0 * 128:r1 * 128]
                               .rearrange("p (c j) -> p c j", j=128),
                               extra_w=[(cDVE, vxn)])
        lw = touts[-1:] + ([(dW, vwl)] if t == 0 else [])
        xTv = xT.rearrange("p (c j) -> p c j", j=128)
        for kc in range(16):
            for ncb in range(3):
                mm(xTv[:, kc, :],
                   ar16[:, 34816 + kc * 1536 + ncb * 512:
                        34816 + kc * 1536 + (ncb + 1) * 512],
                   psA[:, ncb * 512:(ncb + 1) * 512], kc == 0, kc == 15,
                   waits=(lw + [lat_state["lt"]]) if (kc == 0 and ncb == 0) else (),
                   inc=[(cTE, 1)] if (kc == 15 and ncb == 2) else ())
        v_qa = cTE.n
        for kc in range(16):
            mm(xTv[:, kc, :], ar16[:, 59392 + kc * 576: 59392 + kc * 576 + 512],
               psA[:, 1536:2048], kc == 0, kc == 15)
            mm(xTv[:, kc, :], ar16[:, 59392 + kc * 576 + 512: 59392 + (kc + 1) * 576],
               psB[:, 0:64], kc == 0, kc == 15,
               inc=[(cTE, 1)] if kc == 15 else ())
        v_kv = cTE.n
        P.emit("act", lambda e: e.activation(ar32[:, 6144:7680], psA[:, 0:1536],
                                             AF.Square, accum_out=tny[:, 32:33]),
               waits=[(cTE, v_qa)])
        P.emit("act", lambda e: e.activation(ar32[:, 6144:6656], psA[:, 1536:2048],
                                             AF.Square, accum_out=tny[:, 33:34]),
               waits=[(cTE, v_kv)], inc=[(cACT, 1)])
        v_ss = cACT.n
        P.emit("dve", lambda e: e.tensor_scalar(tny[:, 34:35], tny[:, 32:33],
                                                1.0 / 1536, RMS_EPS, ALU.mult,
                                                ALU.add),
               waits=[(cACT, v_ss)])
        P.emit("dve", lambda e: e.tensor_scalar(tny[:, 35:36], tny[:, 33:34],
                                                1.0 / 512, RMS_EPS, ALU.mult,
                                                ALU.add), inc=[(cDVE, 1)])
        v_v = cDVE.n
        P.emit("act", lambda e: e.sqrt(tny[:, 36:38], tny[:, 34:36]),
               waits=[(cDVE, v_v)], inc=[(cACT, 1)])
        v_sq = cACT.n
        P.emit("dve", lambda e: e.reciprocal(tny[:, 38:40], tny[:, 36:38]),
               waits=[(cACT, v_sq)])
        LT = 76800
        lt = ar16[:, LT:LT + LATR]
        P.emit("dve", lambda e: e.tensor_scalar(lt[:, 0:1536], psA[:, 0:1536],
                                                tny[:, 38:39], None, ALU.mult),
               waits=[lat_state["lt"]])
        P.emit("dve", lambda e: e.tensor_scalar(lt[:, 1536:2048], psA[:, 1536:2048],
                                                tny[:, 39:40], None, ALU.mult))
        P.emit("dve", lambda e: e.tensor_copy(lt[:, 2048:2112], psB[:, 0:64]),
               inc=[(cDVE, 1)])
        v_lt = cDVE.n
        transpose_tile(lt, 16,
                       lambda r0, r1, t=t: ar16[:, 0:32768].rearrange(
                           "p (c u) -> p c u", u=2048)[:, r0:r1,
                                                       t * 128:(t + 1) * 128],
                       extra_w=[(cDVE, v_lt)])
        P.emit("te", lambda e: e.transpose(psT[0:64, 0:128], lt[:, 2048:2112],
                                           ident[:, :]),
               waits=[PSTT[0]], inc=[(cTE, 1)])
        v_tt = cTE.n
        lat_state["lt"] = (cTE, v_tt)
        P.emit("dve", lambda e, t=t: e.tensor_copy(
            ar16[0:64, 32768 + t * 128: 32768 + (t + 1) * 128], psT[0:64, 0:128]),
            waits=[(cTE, v_tt)], inc=[(cDVE, 1)])
        PSTT[0] = (cDVE, cDVE.n)
        lat_state["tail"] = PSTT[0]

    contrib_phase(ph3_src, None, latf=ph3_lat, to_contrib=False,
                  src_free=ph3_free)
    dd3a = dma(dOUT, lat_in[0:2048, :].rearrange("(c p) u -> p c u", p=128),
               ar16[:, 0:32768].rearrange("p (c u) -> p c u", u=2048),
               waits=[PSTT[0], lat_state["tail"]])
    dd3b = dma(dNW, lat_in[2048:2112, :], ar16[0:64, 32768:34816],
               waits=[PSTT[0], lat_state["tail"]])
    vcc2 = cc("AllGather", ALU.bypass, lat_in.ap(), lat_out.ap(), sems["cc2"],
              waits=[(dOUT, dd3a), (dNW, dd3b)])
    barrier()

    if tap == "x2":
        _emit_dump(P, sems, x2_dram, y_out, ar16, ar32, F32, waits=[])
        return _finish(nc, P, sems)
    if tap == "lat":
        _emit_dump(P, sems, lat_out, y_out, ar16, ar32, BF16,
                   waits=[(sems["cc2"], vcc2)])
        return _finish(nc, P, sems)

    if tap == "ser4a":
        P.serialize = True
    QBW, KBW, VBW, MSK = 0, 4608, 5632, 6656
    QF1, KF1, QF2, KF2 = 8704, 16896, 25088, 33280
    VSB, KROT = 41472, 49664
    TR = 50176
    S0A, S1A, KPE, COSW, SINW = TR, TR + 8192, TR + 16384, TR + 16896, TR + 25088
    PT2, ATTS = TR, TR + 1024
    OWS, OSL0, OBF0 = TR, TR + 4096, TR + 4608

    for c in range(12):
        dma(dW, ar16[:, QBW + c * 384: QBW + (c + 1) * 384],
            qb_bf[c * 128:(c + 1) * 128, :])
    for c in range(4):
        dma(dW, ar16[:, KBW + c * 256: KBW + (c + 1) * 256],
            kb_bf[c * 128:(c + 1) * 128, :])
        vwq = dma(dW, ar16[:, VBW + c * 256: VBW + (c + 1) * 256],
                  vb_bf[c * 128:(c + 1) * 128, :])
    P.emit("pool", lambda e: e.memset(ar16[:, MSK:MSK + 2048], 0.0))
    for mi in range(4):
        P.emit("pool", lambda e, mi=mi: e.affine_select(
            out=ar16[:, MSK + mi * 512: MSK + (mi + 1) * 512],
            in_=ar16[:, MSK + mi * 512: MSK + (mi + 1) * 512],
            compare_op=ALU.is_ge, fill=NEG, base=-(mi * 128),
            pattern=[[1, 512]], channel_multiplier=-1),
            inc=[(cPOOL, 1)] if mi == 3 else ())
    v_msk = cPOOL.n

    SHUF = []
    for a in range(16):
        SHUF += [2 * a + 1, 2 * a]

    sc_free = [(cACT, 0), (cACT, 0)]
    av_done = [(cDVE, 0), (cDVE, 0)]
    pt_rd = [(cTE, 0), (cTE, 0)]
    cc3_req = [None, None]
    cc3_tok = [None, None]

    def flush_cc3():
        for b_ in range(2):
            if cc3_req[b_] is not None and cc3_tok[b_] is None:
                cc3_tok[b_] = cc("ReduceScatter", ALU.add, o_in_b[b_].ap(),
                                 o_out_b[b_].ap(), sems["cc3"],
                                 waits=[(dOUT, cc3_req[b_])])
    rs3_last = 0
    for pair in range(4):
        hl, bb = pair % 2, pair // 2
        if tap in ("ser4c",):
            P.serialize = True
        allw = [(cTE, cTE.n), (cDVE, cDVE.n), (cACT, cACT.n),
                (cPOOL, cPOOL.n), (dOUT, dOUT.n)]
        cs_tok = (cPOOL, 0)
        for half in range(2):
            d_c = dma(dNW, ar32[0:64, half * 4096:(half + 1) * 4096],
                      cos_pi[:, half * 4096:(half + 1) * 4096],
                      waits=allw + [cs_tok])
            P.emit("pool", lambda e, h=half: e.tensor_copy(
                ar16[0:64, COSW + h * 4096: COSW + (h + 1) * 4096],
                ar32[0:64, h * 4096:(h + 1) * 4096]),
                waits=[(dNW, d_c)] + allw, inc=[(cPOOL, 1)])
            cs_tok = (cPOOL, cPOOL.n)
            d_s = dma(dNW, ar32[0:64, half * 4096:(half + 1) * 4096],
                      sin_pi[:, half * 4096:(half + 1) * 4096], waits=[cs_tok])
            P.emit("pool", lambda e, h=half: e.tensor_copy(
                ar16[0:64, SINW + h * 4096: SINW + (h + 1) * 4096],
                ar32[0:64, h * 4096:(h + 1) * 4096]),
                waits=[(dNW, d_s)], inc=[(cPOOL, 1)])
            cs_tok = (cPOOL, cPOOL.n)
        stage_rd = [(cTE, 0), (cTE, 0)]
        stage_dd = [0, 0]
        kpe_rd = (cPOOL, 0)
        d_k = 0
        ev_tok = (cDVE, 0)
        qp_tok = (cDVE, 0)
        pq_tok = (cPOOL, 0)
        for nb in range(16):
            spar = nb % 2
            rblk, coff = nb // 2, bb * 1024 + (nb % 2) * 512
            stg = ar16[:, TR + spar * 8192: TR + (spar + 1) * 8192]
            exw = [(dSG, dSG.n), stage_rd[spar], (sems["cc2"], vcc2)]
            if nb == 0:
                exw += [cs_tok]
                if pair == 0:
                    exw += [(dW, vwq), (cPOOL, v_msk)]
                else:
                    exw += [sc_free[0], sc_free[1], pt_rd[0], pt_rd[1]]
            d_m = dma(dSG, stg.rearrange("p (c u) -> p c u", u=512),
                      lat_out[rblk * LATR: rblk * LATR + 2048,
                              coff:coff + 512].rearrange("(c p) u -> p c u", p=128),
                      waits=exw)
            stage_dd[spar] = d_m
            d_k = dma(dNW, ar16[0:64, KPE:KPE + 512],
                      lat_out[rblk * LATR + 2048: rblk * LATR + 2112,
                              coff:coff + 512],
                      waits=[(dNW, d_k), kpe_rd] + (exw if nb == 0 else []))
            stgv = stg.rearrange("p (c u) -> p c u", u=512)
            c0 = nb * 512
            for kc in range(12):
                mm(ar16[:, QBW + kc * 384 + hl * QHD:
                        QBW + kc * 384 + hl * QHD + 128],
                   stgv[:, kc, :], psA[:, 0:512], kc == 0, kc == 11,
                   waits=[(dSG, d_m), ev_tok] if kc == 0 else (),
                   inc=[(cTE, 1)] if kc == 11 else ())
            v_q1 = cTE.n
            for kc in range(12):
                mm(ar16[:, QBW + kc * 384 + hl * QHD + 128:
                        QBW + kc * 384 + hl * QHD + QHD],
                   stgv[:, kc, :], psR[0:64, 0:512], kc == 0, kc == 11,
                   waits=[qp_tok] if kc == 0 else (),
                   inc=[(cTE, 1)] if kc == 11 else ())
            v_q2 = cTE.n
            for kc in range(4):
                mm(ar16[:, KBW + kc * 256 + hl * 128:
                        KBW + kc * 256 + (hl + 1) * 128],
                   stgv[:, 12 + kc, :], psA[:, 512:1024], kc == 0, kc == 3,
                   inc=[(cTE, 1)] if kc == 3 else ())
            v_k1 = cTE.n
            for ts in range(4):
                for kc in range(4):
                    mm(stgv[:, 12 + kc, ts * 128:(ts + 1) * 128],
                       ar16[:, VBW + kc * 256 + hl * 128:
                            VBW + kc * 256 + (hl + 1) * 128],
                       psA[:, 1024 + ts * 128: 1024 + (ts + 1) * 128],
                       kc == 0, kc == 3,
                       inc=[(cTE, 1)] if (ts == 3 and kc == 3) else ())
            v_v = cTE.n
            stage_rd[spar] = (cTE, v_v)
            P.emit("dve", lambda e, c0=c0: e.tensor_copy(
                ar16[:, QF1 + c0: QF1 + c0 + 512], psA[:, 0:512]),
                waits=[(cTE, v_q1)])
            P.emit("dve", lambda e, c0=c0: e.tensor_copy(
                ar16[:, KF1 + c0: KF1 + c0 + 512], psA[:, 512:1024]),
                waits=[(cTE, v_k1)])
            P.emit("dve", lambda e, c0=c0: e.tensor_copy(
                ar16[:, VSB + c0: VSB + c0 + 512], psA[:, 1024:1536]),
                waits=[(cTE, v_v)], inc=[(cDVE, 1)])
            ev_tok = (cDVE, cDVE.n)
            P.emit("dve", lambda e: e.tensor_copy(ar32[0:64, 4096:4608],
                                                  psR[0:64, 0:512]),
                   waits=[(cTE, v_q2), pq_tok], inc=[(cDVE, 1)])
            v_qp = cDVE.n
            qp_tok = (cDVE, v_qp)
            P.emit("dve", lambda e: e.stream_shuffle(ar32[0:64, 4608:5120],
                                                     ar32[0:64, 4096:4608], SHUF),
                   inc=[(cDVE, 1)])
            v_qr = cDVE.n
            P.emit("pool", lambda e, c0=c0: e.tensor_tensor(
                ar32[0:64, 5120:5632], ar32[0:64, 4096:4608],
                ar16[0:64, COSW + c0: COSW + c0 + 512], ALU.mult),
                waits=[(cDVE, v_qp)])
            P.emit("pool", lambda e, c0=c0: e.tensor_tensor(
                ar32[0:64, 5632:6144], ar32[0:64, 4608:5120],
                ar16[0:64, SINW + c0: SINW + c0 + 512], ALU.mult),
                waits=[(cDVE, v_qr)])
            P.emit("pool", lambda e, c0=c0: e.tensor_tensor(
                ar16[0:64, QF2 + c0: QF2 + c0 + 512], ar32[0:64, 5120:5632],
                ar32[0:64, 5632:6144], ALU.add), inc=[(cPOOL, 1)])
            pq_tok = (cPOOL, cPOOL.n)
            P.emit("dve", lambda e: e.stream_shuffle(
                ar16[0:64, KROT:KROT + 512], ar16[0:64, KPE:KPE + 512], SHUF),
                waits=[(dNW, d_k)], inc=[(cDVE, 1)])
            v_kr = cDVE.n
            P.emit("pool", lambda e, c0=c0: e.tensor_tensor(
                ar32[0:64, 6144:6656], ar16[0:64, KPE:KPE + 512],
                ar16[0:64, COSW + c0: COSW + c0 + 512], ALU.mult),
                waits=[(dNW, d_k)])
            P.emit("pool", lambda e, c0=c0: e.tensor_tensor(
                ar32[0:64, 6656:7168], ar16[0:64, KROT:KROT + 512],
                ar16[0:64, SINW + c0: SINW + c0 + 512], ALU.mult),
                waits=[(cDVE, v_kr)])
            P.emit("pool", lambda e, c0=c0: e.tensor_tensor(
                ar16[0:64, KF2 + c0: KF2 + c0 + 512], ar32[0:64, 6144:6656],
                ar32[0:64, 6656:7168], ALU.add), inc=[(cPOOL, 1)])
            kpe_rd = (cPOOL, cPOOL.n)
        ew = [ev_tok, pq_tok, kpe_rd, qp_tok]

        P.emit("act", lambda e: e.activation(ar16[:, S0A:S0A + 8192],
                                             ar16[:, QF1:QF1 + 8192], AF.Square),
               waits=ew + [sc_free[0], sc_free[1], pt_rd[0], pt_rd[1]])
        P.emit("act", lambda e: e.activation(ar16[0:64, S1A:S1A + 8192],
                                             ar16[0:64, QF2:QF2 + 8192], AF.Square),
               inc=[(cACT, 1)])
        v_sqq = cACT.n
        nrm_r = (cDVE, 0)
        for c in range(16):
            mm(ones_c[:, 0:1], ar16[:, S0A + c * 512: S0A + (c + 1) * 512],
               psR[0:1, 0:512], True, False,
               waits=([(cACT, v_sqq)] if c == 0 else []) + [nrm_r])
            mm(ones_c[0:64, 0:1], ar16[0:64, S1A + c * 512: S1A + (c + 1) * 512],
               psR[0:1, 0:512], False, True, inc=[(cTE, 1)])
            v_n = cTE.n
            P.emit("dve", lambda e, c=c: e.reduce_max(
                ar32[0:1, 7168 + c: 7169 + c], psR[0:1, 0:512], axis=AX.X),
                waits=[(cTE, v_n)], inc=[(cDVE, 1)])
            nrm_r = (cDVE, cDVE.n)
        P.emit("act", lambda e: e.activation(ar16[:, S0A:S0A + 8192],
                                             ar16[:, KF1:KF1 + 8192], AF.Square),
               waits=[nrm_r])
        P.emit("act", lambda e: e.activation(ar16[0:64, S1A:S1A + 8192],
                                             ar16[0:64, KF2:KF2 + 8192], AF.Square),
               inc=[(cACT, 1)])
        v_sqk = cACT.n
        for c in range(16):
            mm(ones_c[:, 0:1], ar16[:, S0A + c * 512: S0A + (c + 1) * 512],
               psR[0:1, 0:512], True, False,
               waits=([(cACT, v_sqk)] if c == 0 else []) + [nrm_r])
            mm(ones_c[0:64, 0:1], ar16[0:64, S1A + c * 512: S1A + (c + 1) * 512],
               psR[0:1, 0:512], False, True, inc=[(cTE, 1)])
            v_n = cTE.n
            P.emit("dve", lambda e, c=c: e.reduce_max(
                ar32[0:1, 7200 + c: 7201 + c], psR[0:1, 0:512], axis=AX.X),
                waits=[(cTE, v_n)], inc=[(cDVE, 1)])
            nrm_r = (cDVE, cDVE.n)
        P.emit("dve", lambda e: e.reduce_max(ar32[0:1, 7232:7233],
                                             ar32[0:1, 7168:7184], axis=AX.X))
        P.emit("dve", lambda e: e.reduce_max(ar32[0:1, 7233:7234],
                                             ar32[0:1, 7200:7216], axis=AX.X))
        P.emit("dve", lambda e: e.tensor_tensor(ar32[0:1, 7234:7235],
                                                ar32[0:1, 7232:7233],
                                                ar32[0:1, 7233:7234], ALU.mult),
               inc=[(cDVE, 1)])
        v_b2 = cDVE.n
        P.emit("act", lambda e: e.sqrt(ar32[0:1, 7235:7236], ar32[0:1, 7234:7235]),
               waits=[(cDVE, v_b2)], inc=[(cACT, 1)])
        v_bs = cACT.n
        P.emit("dve", lambda e: e.tensor_copy(bcb[0:1, 0:1], ar32[0:1, 7235:7236]),
               waits=[(cACT, v_bs)], inc=[(cDVE, 1)])
        v_bc = cDVE.n
        mm(ones_r[0:1, :], bcb[0:1, 0:1], psR[:, 0:1], True, True,
           waits=[(cDVE, v_bc), nrm_r], inc=[(cTE, 1)])
        v_bct = cTE.n
        P.emit("dve", lambda e: e.tensor_single_scalar(
            ar32[:, 7424:7425], psR[:, 0:1], -1.0, ALU.mult),
            waits=[(cTE, v_bct)], inc=[(cDVE, 1)])
        v_nb = cDVE.n
        negB = ar32[:, 7424:7425]

        if tap == "qk" and pair == 0:
            blocks = ([(QF1 + i * 2048, 128) for i in range(4)] +
                      [(KF1 + i * 2048, 128) for i in range(4)] +
                      [(QF2 + i * 2048, 64) for i in range(4)] +
                      [(KF2 + i * 2048, 64) for i in range(4)])
            prev = 0
            for i, (off, pc) in enumerate(blocks):
                P.emit("dve", lambda e, off=off, pc=pc: e.tensor_copy(
                    ar32[0:pc, 0:2048], ar16[0:pc, off:off + 2048]),
                    waits=list(ew) + [(dOUT, prev)])
                tok = (cDVE, cDVE.n)
                prev = dma(dOUT, y_out[i * 128:i * 128 + pc, :],
                           ar32[0:pc, 0:2048], waits=[tok])
            return _finish(nc, P, sems)

        if tap in ("ser4c",):
            P.serialize = False
        if tap in ("ser4b",):
            P.serialize = True
        flush_cc3()
        av_par = 0
        rsum_evac = (cDVE, 0)
        bc_free = (cDVE, 0)
        att_w = 0
        pend = None
        gpend = None

        def _av(par, kt, nks):
            mm(ar16[:, VSB + kt * 128: VSB + (kt + 1) * 128],
               ar16[:, PT2 + par * 512: PT2 + (par + 1) * 512],
               psB[:, av_par * 512:(av_par + 1) * 512],
               kt == 0, kt == nks - 1,
               waits=[sc_free[par]] + ([av_done[av_par]] if kt == 0 else []))
            mm(ones_c[:, 0:1],
               ar16[:, PT2 + par * 512: PT2 + (par + 1) * 512],
               psR[0:1, 0:512], kt == 0, kt == nks - 1,
               waits=[rsum_evac] if kt == 0 else (), inc=[(cTE, 1)])
            pt_rd[par] = (cTE, cTE.n)

        def _gend(g_old, ap_old):
            nonlocal bc_free, att_w
            mm(ones_r[0:1, :], rs_bf[0:1, 0:512], psA[:, 1024:1536], True, True,
               waits=[rsum_evac, bc_free], inc=[(cTE, 1)])
            v_bcm = cTE.n
            P.emit("dve", lambda e: e.tensor_copy(ar32[:, 4096:4608],
                                                  psA[:, 1024:1536]),
                   waits=[(cTE, v_bcm)], inc=[(cDVE, 1)])
            bc_free = (cDVE, cDVE.n)
            P.emit("dve", lambda e, ap=ap_old: e.tensor_tensor(
                ar16[:, ATTS:ATTS + 512],
                psB[:, ap * 512:(ap + 1) * 512],
                ar32[:, 4096:4608], ALU.mult),
                waits=[(dOUT, att_w)], inc=[(cDVE, 1)])
            av_done[ap_old] = (cDVE, cDVE.n)
            att_w = dma(dOUT, att_t[hl, g_old * 4:(g_old + 1) * 4].rearrange(
                "t p j -> p t j"),
                ar16[:, ATTS:ATTS + 512].rearrange("p (t j) -> p t j", j=128),
                waits=[av_done[ap_old]])

        for g in range(16):
            nks = (g + 1) * 4
            qc = g * 512
            for kt in range(nks):
                par = kt % 2
                diag = kt >= nks - 4
                scw = [sc_free[par]]
                if g == 0 and kt == 0:
                    scw.append((cDVE, v_nb))
                mm(ar16[:, KF1 + kt * 128: KF1 + (kt + 1) * 128],
                   ar16[:, QF1 + qc: QF1 + qc + 512],
                   psA[:, par * 512:(par + 1) * 512], True, False, waits=scw)
                mm(ar16[0:64, KF2 + kt * 128: KF2 + (kt + 1) * 128],
                   ar16[0:64, QF2 + qc: QF2 + qc + 512],
                   psA[:, par * 512:(par + 1) * 512], False, not diag,
                   inc=[(cTE, 1)] if not diag else ())
                if diag:
                    o = kt * 128 - g * 512
                    mm(ident[:, :], ar16[:, MSK + (o // 128) * 512:
                                         MSK + (o // 128 + 1) * 512],
                       psA[:, par * 512:(par + 1) * 512], False, True,
                       inc=[(cTE, 1)])
                v_sc = cTE.n
                P.emit("act", lambda e, par=par: e.activation(
                    ar16[:, PT2 + par * 512: PT2 + (par + 1) * 512],
                    psA[:, par * 512:(par + 1) * 512], AF.Exp,
                    bias=negB[:, 0:1]),
                    waits=[(cTE, v_sc), pt_rd[par]], inc=[(cACT, 1)])
                sc_free[par] = (cACT, cACT.n)
                if gpend is not None and kt == 1:
                    _gend(*gpend)
                    gpend = None
                if pend is not None:
                    _av(pend[0], pend[1], nks)
                pend = (par, kt)
            _av(pend[0], pend[1], nks)
            pend = None
            v_last = cTE.n
            P.emit("dve", lambda e: e.tensor_copy(ar32[0:1, 6144:6656],
                                                  psR[0:1, 0:512]),
                   waits=[(cTE, v_last)])
            P.emit("dve", lambda e: e.reciprocal(ar32[0:1, 6656:7168],
                                                 ar32[0:1, 6144:6656]))
            P.emit("dve", lambda e: e.tensor_copy(rs_bf[0:1, 0:512],
                                                  ar32[0:1, 6656:7168]),
                   inc=[(cDVE, 1)])
            rsum_evac = (cDVE, cDVE.n)
            gpend = (g, av_par)
            av_par ^= 1
        _gend(*gpend)
        gpend = None

        if tap in ("att", "att1") and pair == (0 if tap == "att" else 1):
            prev = 0
            for rb in range(4):
                for tt in range(16):
                    dd = dma(dX, ar16[:, 0:128], att_t[hl, rb * 16 + tt],
                             waits=[(dOUT, dOUT.n), (cDVE, cDVE.n)])
                    P.emit("dve", lambda e, tt=tt: e.tensor_copy(
                        ar32[:, tt * 128:(tt + 1) * 128], ar16[:, 0:128]),
                        waits=[(dX, dd)])
                tok = (cDVE, cDVE.n)
                prev = dma(dOUT, y_out[rb * 128:(rb + 1) * 128, :],
                           ar32[:, 0:2048], waits=[tok])
            return _finish(nc, P, sems)

        if tap in ("ser4b",):
            P.serialize = False
        if tap in ("ser4d",):
            P.serialize = True
        if hl == 1 and tap == "ser4d_end":
            pass
        if hl == 1:
            d_ow1 = dma(dSG, ar16[:, OWS:OWS + 2048],
                        ow_bf[0:128, :],
                        waits=[(dOUT, dOUT.n), (cTE, cTE.n),
                               (cDVE, cDVE.n), (cACT, cACT.n)])
            d_ow2 = dma(dSG, ar16[:, OWS + 2048:OWS + 4096],
                        ow_bf[128:256, :], waits=[(dOUT, dOUT.n)])
            ol_free = [(0, 0), (0, 0)]
            op_free = [(cACT, 0), (cDVE, 0)]
            oe_tok = [(cTE, 0), (cTE, 0)]
            for tt in range(64):
                par = tt % 2
                OSL = OSL0 + par * 256
                d_l0 = dma(dSG, ar16[:, OSL:OSL + 128], att_t[0, tt],
                           waits=[(dSG, dSG.n), oe_tok[par]])
                d_l1 = dma(dSG, ar16[:, OSL + 128:OSL + 256], att_t[1, tt],
                           waits=([(dSG, d_ow2)] if tt < 2 else []) +
                           [oe_tok[par]])
                for ncb in range(4):
                    mm(ar16[:, OSL:OSL + 128],
                       ar16[:, OWS + ncb * 512: OWS + (ncb + 1) * 512],
                       psA[:, ncb * 512:(ncb + 1) * 512], True, False,
                       waits=([(dSG, d_l1), op_free[0],
                               sc_free[0], sc_free[1], bc_free,
                               av_done[0], av_done[1]] if ncb == 0 else
                              ([op_free[1]] if ncb == 2 else ())))
                    mm(ar16[:, OSL + 128:OSL + 256],
                       ar16[:, OWS + 2048 + ncb * 512:
                            OWS + 2048 + (ncb + 1) * 512],
                       psA[:, ncb * 512:(ncb + 1) * 512], False, True,
                       inc=[(cTE, 1)] if ncb == 3 else ())
                v_op = cTE.n
                oe_tok[par] = (cTE, v_op)
                OBF = OBF0 + par * 2048
                P.emit("act", lambda e, OBF=OBF: e.copy(
                    ar16[:, OBF:OBF + 1024], psA[:, 0:1024]),
                    waits=[(cTE, v_op), (dOUT, dOUT.n)], inc=[(cACT, 1)])
                op_free[0] = (cACT, cACT.n)
                P.emit("dve", lambda e, OBF=OBF: e.tensor_copy(
                    ar16[:, OBF + 1024:OBF + 2048], psA[:, 1024:2048]),
                    waits=[(cTE, v_op), (dOUT, dOUT.n)], inc=[(cDVE, 1)])
                op_free[1] = (cDVE, cDVE.n)
                p0 = tt * 128
                dob = dma(dOUT, o_in_b[bb][p0:p0 + 128, :],
                          ar16[:, OBF:OBF + 2048],
                          waits=[op_free[0], op_free[1]])
                ol_free[par] = (d_l1, dob)
            sc_free = [(cACT, cACT.n), (cACT, cACT.n)]
            rs3_last = dOUT.n
            cc3_req[bb] = rs3_last
            if tap in ("ser4d",):
                P.serialize = False

    if tap in ("oin", "ser4a", "ser4b", "ser4c", "ser4d"):
        _emit_dump(P, sems, o_in_b[0], y_out, ar16, ar32, BF16,
                   waits=[(dOUT, rs3_last)])
        return _finish(nc, P, sems)

    flush_cc3()
    barrier()

    if tap == "rs3":
        _emit_dump(P, sems, o_out_b[0], y_out, ar16, ar32, BF16,
                   waits=[(sems["cc3"], cc3_tok[0])])
        return _finish(nc, P, sems)

    ph5_free = {0: None, 1: None}
    ph5_st = {"wr": [0, 0], "rs": [None, None]}

    def ph5_src(t, par):
        xb = ar32[:, par * 2048:(par + 1) * 2048]
        rsb = ar16[:, 78912 + par * 2048: 78912 + (par + 1) * 2048]
        fw = ph5_free[par] or [(cDVE, cDVE.n), (cACT, cACT.n)]
        d_a = dma(dX, xb, x2_dram[t * 128:(t + 1) * 128, :],
                  waits=fw + [(dOUT, ph5_st["wr"][par])])
        tb = t // 8
        d_b = dma(dX, rsb, o_out_b[tb][(t % 8) * 128:(t % 8 + 1) * 128, :],
                  waits=[(sems["cc3"], cc3_tok[tb])] +
                  (ph5_st["rs"][par] or [(cDVE, cDVE.n)]))
        P.emit("dve", lambda e, xb=xb, rsb=rsb: e.scalar_tensor_tensor(
            xb, rsb, 1.0, xb, ALU.mult, ALU.add),
            waits=[(dX, d_b)], inc=[(cDVE, 1)])
        vx3 = cDVE.n
        ph5_st["rs"][par] = [(cDVE, vx3)]
        ph5_st["wr"][par] = dma(dOUT, x3_dram[t * 128:(t + 1) * 128, :], xb,
                                waits=[(cDVE, vx3)])
        return xb, [(cDVE, vx3)]

    cc4_toks = contrib_phase(ph5_src, None, ag_chunks=(ag3_in_c, ag3_out_c),
                             ccsem=sems["cc4"], src_free=ph5_free)
    barrier()

    if tap == "x3":
        _emit_dump(P, sems, x3_dram, y_out, ar16, ar32, F32, waits=[])
        return _finish(nc, P, sems)

    dd6_k = ffn_phase(ag3_out_c, rs4_in_c, sems["cc4"], cc4_toks)
    vcc5_k = [cc("ReduceScatter", ALU.add, rs4_in_c[k].ap(),
                 rs4_out_c[k].ap(), sems["cc5"],
                 waits=[(dOUT, dd6_k[k])]) for k in range(4)]
    barrier()

    if tap == "rs4":
        _emit_dump(P, sems, rs4_out_c[0], y_out, ar16, ar32, BF16,
                   waits=[(sems["cc5"], vcc5_k[0])])
        return _finish(nc, P, sems)

    stt_tok = [(cDVE, 0), (cDVE, 0)]
    ywr = [0, 0]
    for i, (k, tt) in enumerate((k, t) for k in range(4) for t in range(4)):
        par = i % 2
        row0 = k * 512 + tt * 128
        xb = ar32[:, par * 2048:(par + 1) * 2048]
        rsb = ar16[:, 74240 + par * 2048: 74240 + (par + 1) * 2048]
        d_a = dma(dX, xb, x3_dram[row0:row0 + 128, :],
                  waits=[stt_tok[par]] + ([(cDVE, cDVE.n)] if i < 2 else []))
        d_b = dma(dNW, rsb, rs4_out_c[k][tt * 128:(tt + 1) * 128, :],
                  waits=[(sems["cc5"], vcc5_k[k]), stt_tok[par]] +
                  ([(cDVE, cDVE.n)] if i < 2 else []))
        yb = ar32[:, 4096 + par * 2048: 4096 + (par + 1) * 2048]
        P.emit("dve", lambda e, rsb=rsb, xb=xb, yb=yb: e.scalar_tensor_tensor(
            yb, rsb, 0.5, xb, ALU.mult, ALU.add),
            waits=[(dX, d_a), (dNW, d_b), (dOUT, ywr[par])], inc=[(cDVE, 1)])
        stt_tok[par] = (cDVE, cDVE.n)
        ywr[par] = dma(dOUT, y_out[row0:row0 + 128, :],
                       yb, waits=[stt_tok[par]])

    return _finish(nc, P, sems)


def _emit_dump(P, sems, src, y_out, ar16, ar32, sdt, waits=()):
    dX, dOUT, cDVE = sems["dX"], sems["dOUT"], sems["cDVE"]
    rows = min(src.shape[0], 2048)
    cols = min(src.shape[1], 2048)
    prev_out = 0
    for t in range(rows // 128):
        if sdt == F32:
            P.emit("sp", lambda e, t=t: e.dma_start(
                out=y_out[t * 128:(t + 1) * 128, 0:cols],
                in_=src[t * 128:(t + 1) * 128, 0:cols]),
                waits=list(waits), inc=[(dOUT, 16)])
        else:
            P.emit("sp", lambda e, t=t, c=cols: e.dma_start(
                out=ar16[:, 0:c], in_=src[t * 128:(t + 1) * 128, 0:c]),
                waits=list(waits) + [(cDVE, cDVE.n)], inc=[(dX, 16)])
            dd = dX.n
            P.emit("dve", lambda e, c=cols: e.tensor_copy(ar32[:, 0:c],
                                                          ar16[:, 0:c]),
                   waits=[(dX, dd), (dOUT, prev_out)], inc=[(cDVE, 1)])
            vv = cDVE.n
            P.emit("sp", lambda e, t=t, c=cols: e.dma_start(
                out=y_out[t * 128:(t + 1) * 128, 0:c], in_=ar32[:, 0:c]),
                waits=[(cDVE, vv)], inc=[(dOUT, 16)])
            prev_out = dOUT.n


def _finish(nc, P, sems):
    for s in sems.values():
        s.h = nc.alloc_semaphore(s.name)
    P.emit("sp", lambda e: e.nop(),
           waits=[(sems["dOUT"], sems["dOUT"].n)])
    secmap = {"te": "tensor", "act": "scalar", "dve": "vector",
              "pool": "gpsimd", "sp": "sync"}
    with nc.Block() as blk:
        for en in ENGS:
            items = P.items[en]

            def body(e, items=items):
                for fn, ws, ii in items:
                    for s, v in ws:
                        e.wait_ge(s.h, v)
                    inst = fn(e)
                    for s, a in ii:
                        inst = inst.then_inc(s.h, a)
            getattr(blk, secmap[en])(body)
    return nc


def _host_inputs(inp, c):
    f32 = np.float32
    x = np.asarray(inp["L0_attn_hidden_states"], f32)
    qa_w = np.asarray(inp["L0_attn_q_a_w"], f32)
    qa_lnw = np.asarray(inp["L0_attn_q_a_ln_w"], f32)
    qb_w = np.asarray(inp["L0_attn_q_b_w"], f32)
    kva_w = np.asarray(inp["L0_attn_kv_a_w"], f32)
    kva_lnw = np.asarray(inp["L0_attn_kv_a_ln_w"], f32)
    kvb_w = np.asarray(inp["L0_attn_kv_b_w"], f32)
    o_w = np.asarray(inp["L0_attn_o_w"], f32)
    g_w = np.asarray(inp["L0_ffn_gate_weight"], f32)
    g_A = np.asarray(inp["L0_ffn_gate_lora_A"], f32)
    g_B = np.asarray(inp["L0_ffn_gate_lora_B"], f32)
    gup = np.asarray(inp["L0_ffn_gate_up_proj"], f32)
    dwn = np.asarray(inp["L0_ffn_down_proj"], f32)
    gup_A = np.asarray(inp["L0_ffn_gup_lora_A"], f32)
    gup_B = np.asarray(inp["L0_ffn_gup_lora_B"], f32)
    dwn_A = np.asarray(inp["L0_ffn_down_lora_A"], f32)
    dwn_B = np.asarray(inp["L0_ffn_down_lora_B"], f32)

    inv = 1.0 / (10000.0 ** (np.arange(0, ROPE, 2, dtype=np.float64) / ROPE))
    t = np.arange(S, dtype=np.float64)
    fr = np.outer(t, inv)
    cosl = np.cos(fr).astype(f32).T
    sinl = np.sin(fr).astype(f32).T
    cos_pi = np.repeat(cosl, 2, axis=0)
    sin_pi = np.empty((ROPE, S), f32)
    sin_pi[0::2] = -sinl
    sin_pi[1::2] = sinl

    qb3 = qb_w.reshape(H, QHD, 1536)[2 * c:2 * c + 2]
    kb3 = kvb_w.reshape(H, NOPE + VD, 512)[2 * c:2 * c + 2]
    return {
        "x_in": np.ascontiguousarray(
            np.concatenate([x[0, HSTR * c:HSTR * (c + 1)],
                            x[1, HSTR * c:HSTR * (c + 1)]], axis=0)),
        "qa_wT": np.ascontiguousarray(qa_w.T),
        "qa_ln": np.ascontiguousarray(qa_lnw.reshape(12, 128)),
        "kva_wT": np.ascontiguousarray(kva_w.T),
        "kva_ln": np.ascontiguousarray(kva_lnw.reshape(4, 128)),
        "qb_sT": np.ascontiguousarray(qb3.transpose(2, 0, 1).reshape(1536, 2 * QHD)),
        "kb_sT": np.ascontiguousarray(
            kb3[:, :NOPE].transpose(2, 0, 1).reshape(512, 2 * NOPE)),
        "vb_sT": np.ascontiguousarray(
            kb3[:, NOPE:].transpose(2, 0, 1).reshape(512, 2 * VD)),
        "ow_sT": np.ascontiguousarray(o_w[:, 2 * VD * c:2 * VD * (c + 1)].T),
        "gate_wT": np.ascontiguousarray(g_w.T),
        "gate_wmT": np.ascontiguousarray(g_w[c].reshape(16, 128).T),
        "gate_A": g_A,
        "gate_BT": np.ascontiguousarray(g_B.T),
        "gate_mBT": np.ascontiguousarray(g_B[c:c + 1].T),
        "gup_w": gup[c],
        "gup_A": gup_A[c],
        "gup_BT": np.ascontiguousarray(gup_B[c].T),
        "down_w": dwn[c],
        "down_A": dwn_A[c],
        "down_BT": np.ascontiguousarray(dwn_B[c].T),
        "cos_pi": cos_pi,
        "sin_pi": sin_pi,
    }


def kernel(**inputs):
    global _LAST
    nc = build_nc(tap=_TAP)
    in_maps = [_host_inputs(inputs, c) for c in range(NC)]
    res = run_bass_kernel_spmd(nc, in_maps, core_ids=list(range(NC)))
    _LAST = res
    y = np.empty((B, S, D), np.float32)
    for c in range(NC):
        r = res.results[c]["y_out"]
        y[0, HSTR * c:HSTR * (c + 1)] = r[:HSTR]
        y[1, HSTR * c:HSTR * (c + 1)] = r[HSTR:]
    return y
